# version 30
# baseline (speedup 1.0000x reference)
import contextlib
import numpy as np

import concourse.bass as bass
import concourse.mybir as mybir
from concourse.bass_utils import run_bass_kernel_spmd

F32 = mybir.dt.float32
BF16 = mybir.dt.bfloat16
I8 = mybir.dt.int8
AF = mybir.ActivationFunctionType
ALU = mybir.AluOpType
AX = mybir.AxisListType.X

PI = float(np.pi)
TWO_PI = float(2.0 * np.pi)
HALF_PI = float(np.pi / 2.0)
MAGIC = 12582912.0
EPS_LN = 1e-5
EPS_Q = 1e-5

D = 1024
H = 16
NB = 2
TOK = 4096
NT = 32
NG = 8
GT = 4
NTILES = NB * NT

ENGINES = ("sync", "vector", "scalar", "tensor", "gpsimd")
_REPLICA_GROUPS = [list(range(8))]
SEMTAGS = ("X0", "X1", "R0", "R1", "O0", "O1", "W", "T", "S", "V", "C", "G")


class Prog:
    def __init__(self):
        self.q = {e: [] for e in ENGINES}
        self.cnt = {t: 0 for t in SEMTAGS}
        self.sems = {}
        self.meta = {}
        self.last_inc_eng = {}
        self.cur_eng = None

    def em(self, eng, fn, meta=None):
        self.q[eng].append(fn)
        self.meta.setdefault(eng, []).append(meta)

    def inc(self, tag, n=1):
        self.cnt[tag] += n
        self.last_inc_eng[tag] = self.cur_eng
        return self.cnt[tag]

    def wait(self, eng, tag, upto):
        if upto > 0:
            s = self.sems[tag]
            self.em(eng, lambda e, s=s, v=upto: e.wait_ge(s, v), meta=("wait", tag, upto))

    def waits(self, eng, guard):
        for tag, v in guard.items():
            self.wait(eng, tag, v)


def build_nc():
    nc = bass.Bass("TRN2", target_bir_lowering=False, dynamic_dma_scratch_size=8192,
                   detect_race_conditions=False)

    x_h = nc.dram_tensor("x", [NB * TOK, D], F32, kind="ExternalInput")
    wq_h = nc.dram_tensor("wq", [128, 8 * D], F32, kind="ExternalInput")
    wk_h = nc.dram_tensor("wk", [128, 8 * D], F32, kind="ExternalInput")
    wv_h = nc.dram_tensor("wv", [128, 8 * D], F32, kind="ExternalInput")
    wo_h = nc.dram_tensor("wo", [128, 8 * D], F32, kind="ExternalInput")
    wf_h = nc.dram_tensor("wf", [128, 8 * D], I8, kind="ExternalInput")
    proj_h = nc.dram_tensor("proj", [128, 128], F32, kind="ExternalInput")
    bq_h = nc.dram_tensor("bq", [128, 8], F32, kind="ExternalInput")
    bk_h = nc.dram_tensor("bk", [128, 8], F32, kind="ExternalInput")
    bv_h = nc.dram_tensor("bv", [1, D], F32, kind="ExternalInput")
    bo_h = nc.dram_tensor("bo", [1, D], F32, kind="ExternalInput")
    fb_h = nc.dram_tensor("fb", [1, D], F32, kind="ExternalInput")
    wsc_h = nc.dram_tensor("wsc", [1, D], F32, kind="ExternalInput")
    id_h = nc.dram_tensor("id_h", [128, 128], F32, kind="ExternalInput")
    y_h = nc.dram_tensor("y", [NB * TOK, D], F32, kind="ExternalOutput")
    cc_in = nc.dram_tensor("cc_in", [1, 8], F32)
    cc_out = nc.dram_tensor("cc_out", [1, 8], F32)

    S = contextlib.ExitStack()
    sb = lambda name, shape, dt: S.enter_context(nc.sbuf_tensor(name, shape, dt))
    psum = lambda name, shape: S.enter_context(nc.psum_tensor(name, shape, F32))

    with S:
        wA_s = sb("wA_s", [128, 8 * D], BF16)
        wB_s = sb("wB_s", [128, 8 * D], BF16)
        wf_i = sb("wf_i", [128, D], I8)
        wf_s = sb("wf_s", [128, 8 * D], BF16)
        proj_f = sb("proj_f", [128, 128], F32)
        proj_b = sb("proj_b", [128, 128], BF16)
        proj_z = [sb("proj_z0", [128, 128], BF16), sb("proj_z1", [128, 128], BF16)]
        maskh = [sb("maskh0", [128, 1], F32), sb("maskh1", [128, 1], F32)]
        id_f = sb("id_f", [128, 128], F32)
        id_b = sb("id_b", [128, 128], BF16)
        bq_s = sb("bq_s", [128, 8], F32)
        bk_s = sb("bk_s", [128, 8], F32)
        bias_pack = sb("bias_pack", [65, 2 * D], F32)
        ones_r = sb("ones_r", [65, 128], F32)
        nph_c = sb("nph_c", [128, 1], F32)
        npi_c = sb("npi_c", [128, 1], F32)
        c128_c = sb("c128_c", [128, 1], F32)
        eighth_c = sb("eighth_c", [128, 1], F32)
        st_sum = sb("st_sum", [128, NTILES], F32)
        st_ssq = sb("st_ssq", [128, NTILES], F32)
        st_max = sb("st_max", [128, NTILES], F32)
        st_min = sb("st_min", [128, NTILES], F32)
        negm = sb("negm", [128, NTILES], F32)
        rstd = sb("rstd", [128, NTILES], F32)
        negmr = sb("negmr", [128, NTILES], F32)
        tmp64a = sb("tmp64a", [128, NTILES], F32)
        tmp64b = sb("tmp64b", [128, NTILES], F32)
        maxcol = sb("maxcol", [128, 1], F32)
        mrow = sb("mrow", [1, 128], F32)
        lmax = sb("lmax", [1, 1], F32)
        s_sc = sb("s_sc", [1, 1], F32)
        invs_sc = sb("invs_sc", [1, 1], F32)
        invs_c = sb("invs_c", [128, 1], F32)
        gs8 = sb("gs8", [1, 8], F32)
        sg8 = sb("sg8", [1, 8], F32)
        swsc_b = sb("swsc_b", [128, D], F32)
        bvb_b = sb("bvb_b", [128, D], BF16)
        cb_b = sb("cb_b", [128, D], BF16)
        xin = [sb("xin0", [128, D], F32), sb("xin1", [128, D], F32)]
        xres = [sb("xres0", [128, D], F32), sb("xres1", [128, D], F32)]
        xnbf = [sb("xnbf0", [128, D], BF16), sb("xnbf1", [128, D], BF16)]
        xnB = sb("xnB", [128, 8 * 512], BF16)
        kqB = sb("kqB", [128, 8 * 512], BF16)
        kpA = sb("kpA", [128, H * 512], BF16)
        vA = [sb("vA0", [128, D], BF16), sb("vA1", [128, D], BF16),
              sb("vA2", [128, D], BF16), sb("vA3", [128, D], BF16)]
        pband = [sb(f"pband{j}", [128, 1024], BF16) for j in range(4)]
        ywr = [sb(f"ywr{j}", [128, 1024], BF16) for j in range(4)]
        cwr = [sb(f"cwr{j}", [128, 1024], BF16) for j in range(4)]
        yar = [sb(f"yar{j}", [128, 1024], BF16) for j in range(4)]
        kv_s = sb("kv_s", [128, H * 128], BF16)
        qp_cos = [sb(f"qpc{j}", [128, 1024], BF16) for j in range(3)]
        qp_sin = [sb(f"qps{j}", [128, 1024], BF16) for j in range(3)]
        attnT = [sb("attnT0", [128, 8 * 512], BF16), sb("attnT1", [128, 8 * 512], BF16)]
        xi8B = sb("xi8B", [128, 8 * 512], BF16)
        rtmp = sb("rtmp", [128, 512], F32)
        woS = sb("woS", [128, 1024], BF16)
        xout = [sb("xout0", [128, D], F32), sb("xout1", [128, D], F32)]

        pT = psum("pT", [128, 1024])
        pQK = psum("pQK", [128, 512])
        pP = psum("pP", [128, 512])
        pKV = psum("pKV", [128, 2048])

        P = Prog()
        P.sems = {t: S.enter_context(nc.semaphore("s" + t)) for t in SEMTAGS}

        GUARD = {}

        def g_wait(eng, name):
            P.waits(eng, GUARD.get(name, {}))

        def g_read(name, tag, cnt):
            GUARD.setdefault(name, {})[tag] = cnt

        for src, dst in ((wk_h, wA_s[:, :]), (wv_h, wB_s[:, :]),
                         (proj_h, proj_f[:, :]), (id_h, id_f[:, :]),
                         (bq_h, bq_s[:, :]), (bk_h, bk_s[:, :]),
                         (bv_h, bias_pack[64:65, 0:D]), (bo_h, bias_pack[32:33, D:2 * D]),
                         (fb_h, bias_pack[32:33, 0:D]), (wsc_h, bias_pack[0:1, 0:D])):
            P.em("gpsimd", lambda g, a=src, b=dst: g.dma_start(out=b, in_=a[:, :]).then_inc(P.sems["W"], 16))
            P.inc("W", 16)
        wfv = 0
        for ch in range(8):
            if ch >= 1:
                P.wait("gpsimd", "V", wfv)
            P.em("gpsimd", lambda g, ch=ch: g.dma_start(
                out=wf_i[:, :], in_=wf_h[:, ch * D:(ch + 1) * D]).then_inc(P.sems["W"], 16))
            wc = P.inc("W", 16)
            P.wait("vector", "W", wc)
            P.em("vector", lambda v, ch=ch: v.tensor_copy(
                out=wf_s[:, ch * D:(ch + 1) * D], in_=wf_i[:, :]).then_inc(P.sems["V"], 1))
            wfv = P.inc("V")
        w0 = P.cnt["W"]

        P.em("vector", lambda v: v.memset(ones_r[:], 1.0))
        P.em("vector", lambda v: v.memset(nph_c[:], -HALF_PI))
        P.em("vector", lambda v: v.memset(npi_c[:], -PI))
        P.em("vector", lambda v: v.memset(c128_c[:], 1.0 / 128.0))
        P.em("vector", lambda v: v.memset(eighth_c[:], 0.125))
        P.wait("vector", "W", w0)
        P.em("vector", lambda v: v.tensor_copy(out=id_b[:], in_=id_f[:]).then_inc(P.sems["V"], 1))
        v_const = P.inc("V")
        P.wait("scalar", "V", v_const)
        P.wait("scalar", "W", w0)
        P.em("scalar", lambda s: s.activation(out=proj_b[:], in_=proj_f[:], func=AF.Identity,
                                              scale=eighth_c[:, 0:1]).then_inc(P.sems["S"], 1))
        s_const = P.inc("S")
        P.em("vector", lambda v: v.memset(maskh[0][0:64, 0:1], 1.0))
        P.em("vector", lambda v: v.memset(maskh[0][64:128, 0:1], 0.0))
        P.em("vector", lambda v: v.memset(maskh[1][0:64, 0:1], 0.0))
        P.em("vector", lambda v: v.memset(maskh[1][64:128, 0:1], 1.0))
        P.wait("vector", "S", s_const)
        for zz in range(2):
            P.em("vector", lambda v, zz=zz: v.tensor_scalar(
                out=proj_z[zz][:, :], in0=proj_b[:, :], scalar1=maskh[zz][:, 0:1],
                scalar2=None, op0=ALU.mult).then_inc(P.sems["V"], 1))
            P.inc("V")

        def vstep(fn):
            P.em("vector", lambda v, fn=fn: fn(v).then_inc(P.sems["V"], 1))
            c = P.inc("V")
            P.wait("vector", "V", c)
            return c

        P.wait("vector", "W", w0)
        vstep(lambda v: v.tensor_add(out=bias_pack[32:33, 0:D], in0=bias_pack[32:33, 0:D],
                                     in1=bias_pack[32:33, D:2 * D]))
        bv0 = P.cnt["V"]
        P.wait("tensor", "V", bv0)
        P.wait("tensor", "W", w0)
        P.em("tensor", lambda t: t.matmul(out=pT[0:128, 0:512], lhsT=ones_r[64:65, 0:128],
                                          rhs=bias_pack[64:65, 0:512], start=True, stop=True))
        P.em("tensor", lambda t: t.matmul(out=pT[0:128, 512:1024], lhsT=ones_r[64:65, 0:128],
                                          rhs=bias_pack[64:65, 512:1024], start=True, stop=True))
        P.em("tensor", lambda t: t.matmul(out=pQK[0:128, 0:512], lhsT=ones_r[32:33, 0:128],
                                          rhs=bias_pack[32:33, 0:512], start=True, stop=True))
        P.em("tensor", lambda t: t.matmul(out=pP[0:128, 0:512], lhsT=ones_r[32:33, 0:128],
                                          rhs=bias_pack[32:33, 512:1024], start=True, stop=True).then_inc(P.sems["T"], 1))
        bt3 = P.inc("T")
        P.wait("vector", "T", bt3)
        vstep(lambda v: v.tensor_copy(out=bvb_b[:, :], in_=pT[:, 0:1024]))
        vstep(lambda v: v.tensor_copy(out=cb_b[:, 0:512], in_=pQK[:, 0:512]))
        bv9 = vstep(lambda v: v.tensor_copy(out=cb_b[:, 512:1024], in_=pP[:, 0:512]))
        GUARD["pT"] = {"V": bv9}
        GUARD["pQK"] = {"V": bv9}
        GUARD["pP"] = {"V": bv9}

        a_sc = [0] * NTILES
        a_v = [0] * NTILES

        def emit_stat_tile(i, buf, rtag):
            par = i % 2
            gname = ("xin%d" % par) if buf is xin else ("xres%d" % par)
            g_wait("sync", gname)
            P.em("sync", lambda e, i=i, par=par, buf=buf: e.dma_start(
                out=buf[par][:, :], in_=x_h[i * 128:(i + 1) * 128, :]).then_inc(P.sems[rtag % par], 16))
            xc = P.inc(rtag % par, 16)
            P.wait("scalar", rtag % par, xc)
            if i >= 2:
                P.wait("scalar", "S", a_sc[i - 2])
            P.em("scalar", lambda s, i=i, par=par, buf=buf: s.activation(
                out=xout[par][:, :], in_=buf[par][:, :], func=AF.Square,
                accum_out=st_ssq[:, i:i + 1]))
            P.em("scalar", lambda s, i=i, par=par, buf=buf: s.activation(
                out=xout[par][:, :], in_=buf[par][:, :], func=AF.Identity,
                accum_out=st_sum[:, i:i + 1]).then_inc(P.sems["S"], 1))
            a_sc[i] = P.inc("S")
            P.wait("vector", rtag % par, xc)
            P.em("vector", lambda v, i=i, par=par, buf=buf: v.tensor_reduce(
                out=st_max[:, i:i + 1], in_=buf[par][:, :], axis=AX, op=ALU.max).then_inc(P.sems["V"], 1))
            P.inc("V")
            P.em("vector", lambda v, i=i, par=par, buf=buf: v.tensor_reduce(
                out=st_min[:, i:i + 1], in_=buf[par][:, :], axis=AX, op=ALU.min).then_inc(P.sems["V"], 1))
            a_v[i] = P.inc("V")
            GUARD[gname] = {"S": a_sc[i], "V": a_v[i]}

        for i in range(NT):
            emit_stat_tile(i, xin, "X%d")

        def emit_B1(c0, c1):
            P.wait("vector", "S", a_sc[c1 - 1])
            P.wait("vector", "V", a_v[c1 - 1])
            sl = slice(c0, c1)
            vstep(lambda v: v.tensor_scalar(out=negm[:, sl], in0=st_sum[:, sl], scalar1=-1.0 / D,
                                            scalar2=None, op0=ALU.mult))
            vstep(lambda v: v.tensor_mul(out=tmp64a[:, sl], in0=negm[:, sl], in1=negm[:, sl]))
            vstep(lambda v: v.scalar_tensor_tensor(out=tmp64b[:, sl], in0=st_ssq[:, sl], scalar=1.0 / D,
                                                   in1=tmp64a[:, sl], op0=ALU.mult, op1=ALU.subtract))
            bv1 = vstep(lambda v: v.tensor_scalar(out=tmp64b[:, sl], in0=tmp64b[:, sl], scalar1=EPS_LN,
                                                  scalar2=None, op0=ALU.add))
            P.wait("scalar", "V", bv1)
            P.em("scalar", lambda s, sl=sl: s.activation(out=tmp64a[:, sl], in_=tmp64b[:, sl],
                                                         func=AF.Sqrt).then_inc(P.sems["S"], 1))
            bs1 = P.inc("S")
            P.wait("vector", "S", bs1)
            vstep(lambda v: v.reciprocal(out=rstd[:, sl], in_=tmp64a[:, sl]))
            vstep(lambda v: v.tensor_mul(out=negmr[:, sl], in0=negm[:, sl], in1=rstd[:, sl]))
            vstep(lambda v: v.tensor_add(out=tmp64a[:, sl], in0=st_max[:, sl], in1=negm[:, sl]))
            vstep(lambda v: v.tensor_add(out=tmp64b[:, sl], in0=st_min[:, sl], in1=negm[:, sl]))
            vstep(lambda v: v.tensor_scalar(out=tmp64b[:, sl], in0=tmp64b[:, sl], scalar1=-1.0,
                                            scalar2=None, op0=ALU.mult))
            vstep(lambda v: v.tensor_max(out=tmp64a[:, sl], in0=tmp64a[:, sl], in1=tmp64b[:, sl]))
            return vstep(lambda v: v.tensor_mul(out=tmp64a[:, sl], in0=tmp64a[:, sl], in1=rstd[:, sl]))

        b1v = emit_B1(0, NT)
        P.wait("scalar", "V", b1v)

        def emit_B2():
            bv2 = vstep(lambda v: v.tensor_reduce(out=maxcol[:, 0:1], in_=tmp64a[:, 0:NTILES],
                                                  axis=AX, op=ALU.max))
            P.wait("tensor", "V", bv2)
            g_wait("tensor", "pT")
            P.em("tensor", lambda t: t.matmul(out=pT[0:1, 0:128], lhsT=maxcol[:, 0:1], rhs=id_f[:, :],
                                              start=True, stop=True).then_inc(P.sems["T"], 1))
            bt1 = P.inc("T")
            P.wait("vector", "T", bt1)
            vstep(lambda v: v.tensor_copy(out=mrow[:], in_=pT[0:1, 0:128]))
            vstep(lambda v: v.tensor_reduce(out=lmax[0:1, 0:1], in_=mrow[0:1, 0:128], axis=AX, op=ALU.max))
            bv4 = vstep(lambda v: v.tensor_scalar(out=gs8[0:1, 0:8], in0=ones_r[0:1, 0:8],
                                                  scalar1=lmax[0:1, 0:1], scalar2=None, op0=ALU.mult))
            P.wait("gpsimd", "V", bv4)
            P.em("gpsimd", lambda g: g.dma_start(out=cc_in[:, :], in_=gs8[:, :]).then_inc(P.sems["W"], 16))
            w1 = P.inc("W", 16)
            P.wait("gpsimd", "W", w1)
            P.em("gpsimd", lambda g: g.collective_compute(
                "AllReduce", ALU.max, replica_groups=_REPLICA_GROUPS,
                ins=[cc_in.ap().opt()], outs=[cc_out.ap().opt()]).then_inc(P.sems["C"], 1))
            cc1 = P.inc("C")
            P.wait("gpsimd", "C", cc1)
            P.em("gpsimd", lambda g: g.dma_start(out=sg8[:, :], in_=cc_out[:, :]).then_inc(P.sems["W"], 16))
            w2 = P.inc("W", 16)
            P.wait("vector", "W", w2)
            vstep(lambda v: v.tensor_scalar(out=s_sc[0:1, 0:1], in0=sg8[0:1, 0:1],
                                            scalar1=1.0 / 127.0, scalar2=EPS_Q,
                                            op0=ALU.mult, op1=ALU.max))
            vstep(lambda v: v.reciprocal(out=invs_sc[0:1, 0:1], in_=s_sc[0:1, 0:1]))
            vstep(lambda v: v.tensor_scalar(out=bias_pack[0:1, D:2 * D], in0=bias_pack[0:1, 0:D],
                                            scalar1=s_sc[0:1, 0:1], scalar2=None, op0=ALU.mult))
            bv8 = P.cnt["V"]
            P.wait("tensor", "V", bv8)
            g_wait("tensor", "pQK")
            P.em("tensor", lambda t: t.matmul(out=pQK[0:128, 0:1], lhsT=ones_r[0:1, 0:128],
                                              rhs=invs_sc[0:1, 0:1], start=True, stop=True))
            P.em("tensor", lambda t: t.matmul(out=pT[0:128, 0:512], lhsT=ones_r[0:1, 0:128],
                                              rhs=bias_pack[0:1, D:D + 512], start=True, stop=True))
            P.em("tensor", lambda t: t.matmul(out=pT[0:128, 512:1024], lhsT=ones_r[0:1, 0:128],
                                              rhs=bias_pack[0:1, D + 512:D + 1024],
                                              start=True, stop=True).then_inc(P.sems["T"], 1))
            bt2 = P.inc("T")
            P.wait("vector", "T", bt2)
            vstep(lambda v: v.tensor_copy(out=invs_c[:, 0:1], in_=pQK[:, 0:1]))
            bv9a = vstep(lambda v: v.tensor_copy(out=swsc_b[:, :], in_=pT[:, 0:1024]))
            GUARD["pT"] = {"V": bv9a}
            GUARD["pQK"] = {"V": bv9a}
            P.wait("scalar", "V", bv9a)

        sc_xn = [0] * NTILES
        xnb_ev = {}
        xld = {}

        def emit_x_load(i):
            par = i % 2
            g_wait("sync", f"xin{par}")
            P.em("sync", lambda e, i=i, par=par: e.dma_start(
                out=xin[par][:, :], in_=x_h[i * 128:(i + 1) * 128, :]).then_inc(P.sems["X%d" % par], 16))
            xld[i] = P.inc("X%d" % par, 16)
            return xld[i]

        def emit_xn_tile(i, ti):
            par = i % 2
            xc = xld.pop(i, None)
            if xc is None:
                xc = emit_x_load(i)
                del xld[i]
            P.wait("vector", "X%d" % par, xc)
            g_wait("vector", f"xnbf{par}")
            P.em("vector", lambda v, i=i, par=par: v.tensor_scalar(
                out=xnbf[par][:, :], in0=xin[par][:, :],
                scalar1=rstd[:, i:i + 1], scalar2=negmr[:, i:i + 1],
                op0=ALU.mult, op1=ALU.add).then_inc(P.sems["V"], 1))
            sc_xn[i] = P.inc("V")
            GUARD[f"xin{par}"] = {"V": sc_xn[i]}
            P.wait("tensor", "V", sc_xn[i])
            g_wait("tensor", "pQK")
            g_wait("tensor", "pP")
            for c in range(8):
                def mmT(t, par=par, c=c, last=(c == 7)):
                    bank = pQK if c < 4 else pP
                    r = t.matmul(out=bank[:, (c % 4) * 128:(c % 4) * 128 + 128],
                                 lhsT=xnbf[par][:, c * 128:(c + 1) * 128],
                                 rhs=id_b[:, :], start=True, stop=True)
                    if last:
                        r.then_inc(P.sems["T"], 1)
                P.em("tensor", mmT)
            tc = P.inc("T")
            GUARD[f"xnbf{par}"] = {"T": tc}
            P.wait("scalar", "T", tc)
            g_wait("scalar", "xnB_sc")
            P.em("scalar", lambda s, ti=ti: s.activation(
                out=bass.AP(xnB, ti * 128, [[8 * 512, 128], [512, 4], [1, 128]]),
                in_=pQK[:, 0:512], func=AF.Copy).then_inc(P.sems["S"], 1))
            e_s = P.inc("S")
            P.wait("vector", "T", tc)
            g_wait("vector", "xnB_v")
            P.em("vector", lambda v, ti=ti: v.tensor_copy(
                out=bass.AP(xnB, 4 * 512 + ti * 128, [[8 * 512, 128], [512, 4], [1, 128]]),
                in_=pP[:, 0:512]).then_inc(P.sems["V"], 1))
            e_v = P.inc("V")
            GUARD["pQK"] = {"S": e_s}
            GUARD["pP"] = {"V": e_v}
            return e_s, e_v

        def emit_kq_group(w_s, b_s, xnb_done, wl=0):
            P.wait("tensor", "S", xnb_done[0])
            P.wait("tensor", "V", xnb_done[1])
            P.wait("tensor", "W", wl)
            banks = [(pQK, 0, "pQK"), (pP, 0, "pP"), (pT, 0, "pT"), (pT, 512, None)]
            ev = {}
            for jc in range(8):
                bank, off, bname = banks[jc % 4]
                if jc >= 4:
                    P.wait("tensor", "S", ev[jc - 4])
                elif bname is not None:
                    g_wait("tensor", bname)
                for dc in range(8):
                    def mmK(t, jc=jc, dc=dc, bank=bank, off=off, w_s=w_s, last=(dc == 7)):
                        r = t.matmul(out=bank[:, off:off + 512],
                                     lhsT=w_s[:, dc * D + jc * 128: dc * D + jc * 128 + 128],
                                     rhs=xnB[:, dc * 512:(dc + 1) * 512],
                                     start=(dc == 0), stop=(dc == 7))
                        if last:
                            r.then_inc(P.sems["T"], 1)
                    P.em("tensor", mmK)
                tc = P.inc("T")
                P.wait("scalar", "T", tc)
                g_wait("scalar", "kqB")
                def evK(s, jc=jc, bank=bank, off=off, b_s=b_s):
                    s.activation(out=kqB[:, jc * 512:(jc + 1) * 512], in_=bank[:, off:off + 512],
                                 func=AF.Identity, bias=b_s[:, jc:jc + 1]).then_inc(P.sems["S"], 1)
                P.em("scalar", evK)
                ev[jc] = P.inc("S")
            GUARD["pQK"] = {"S": ev[4]}
            GUARD["pP"] = {"S": ev[5]}
            GUARD["pT"] = {"S": ev[7]}
            return ev

        wload_cnt = {"A": 0, "B": 0}

        def emit_wload(which, src_h):
            dst = wA_s if which == "A" else wB_s
            P.wait("gpsimd", "T", P.cnt["T"])
            P.em("gpsimd", lambda g, s2=src_h, d2=dst: g.dma_start(
                out=d2[:, :], in_=s2[:, :]).then_inc(P.sems["W"], 16))
            wload_cnt[which] = P.inc("W", 16)
            return wload_cnt[which]

        kv_evac = {"S": 0}

        def emit_C(b, stat_it=None):
            import collections
            kvq = collections.deque()
            snq = collections.deque()
            kv_round = {}
            sin_done = {}
            grc = [0]
            if b == 1:
                emit_wload("B", wv_h)
            wlB = wload_cnt["B"]
            def emit_V_mm(ti):
                vsl = ti % 4
                g_wait("tensor", f"vA{vsl}")
                g_wait("tensor", "pT")
                tcs = []
                for jc2 in range(2):
                    off = jc2 * 512
                    for dc in range(8):
                        def mmV(t, ti=ti, jc2=jc2, dc=dc, off=off, last=(dc == 7)):
                            r2 = t.matmul(out=pT[:, off:off + 512],
                                          lhsT=xnB[:, dc * 512 + ti * 128: dc * 512 + ti * 128 + 128],
                                          rhs=wB_s[:, dc * D + jc2 * 512: dc * D + jc2 * 512 + 512],
                                          start=(dc == 0), stop=(dc == 7))
                            if last:
                                r2.then_inc(P.sems["T"], 1)
                        P.em("tensor", mmV)
                    tcs.append(P.inc("T"))
                return tcs

            def emit_V_ev(ti, tcs):
                vsl = ti % 4
                for jc2 in range(2):
                    P.wait("vector", "T", tcs[jc2])
                    def evV(v, jc2=jc2, off=jc2 * 512, vsl=vsl):
                        v.tensor_add(out=vA[vsl][:, jc2 * 512:(jc2 + 1) * 512],
                                     in0=pT[:, off:off + 512],
                                     in1=bvb_b[:, jc2 * 512:(jc2 + 1) * 512]).then_inc(P.sems["V"], 1)
                    P.em("vector", evV)
                    P.inc("V")
                GUARD["pT"] = {"V": P.cnt["V"]}
                return P.cnt["V"]

            for g in range(NG):
                i0 = b * NT + g * GT
                xd = None
                for ti in range(GT):
                    xd = emit_xn_tile(i0 + ti, ti)
                emit_kq_group(wA_s, bk_s, xd, wl=wload_cnt["A"])
                if g == NG - 1:
                    emit_wload("A", wq_h)
                if g + 1 < NG:
                    emit_x_load(i0 + GT)
                    emit_x_load(i0 + GT + 1)
                if g == 0:
                    P.wait("tensor", "W", wlB)
                for ti in range(GT):
                    i = i0 + ti
                    vsl = ti % 4
                    v_done = emit_V_ev(ti, emit_V_mm(ti))
                    for r in range(2):
                        gr = grc[0]; grc[0] += 1
                        rb = gr % 2
                        ksl = gr % 4
                        wsl = gr % 4
                        use_pt = (rb == 0) or (gr % 8 == 7)
                        if use_pt:
                            g_wait("tensor", "pT")
                            pdst = [(pT, 0), (pT, 512)]
                        else:
                            g_wait("tensor", "pQK")
                            g_wait("tensor", "pP")
                            pdst = [(pQK, 0), (pP, 0)]
                        for hh in range(8):
                            h = r * 8 + hh
                            bank, off = pdst[hh // 4]
                            def mmP(t, h=h, hh=hh, ti=ti, bank=bank, off=off, last=(hh == 7)):
                                r2 = t.matmul(out=bank[:, off + (hh % 4) * 128: off + (hh % 4) * 128 + 128],
                                              lhsT=kqB[:, (h // 2) * 512 + ti * 128:(h // 2) * 512 + ti * 128 + 128],
                                              rhs=proj_z[h % 2][:, :],
                                              start=True, stop=True)
                                if last:
                                    r2.then_inc(P.sems["T"], 1)
                            P.em("tensor", mmP)
                        tc = P.inc("T")
                        if len(kvq) >= 4:
                            kvq.popleft()()
                        P.wait("scalar", "T", tc)
                        g_wait("scalar", f"pband{wsl}")
                        if use_pt:
                            P.em("scalar", lambda s, wsl=wsl: s.activation(
                                out=pband[wsl][:, :], in_=pT[:, 0:1024], func=AF.Identity,
                                bias=npi_c[:, 0:1]).then_inc(P.sems["S"], 1))
                            pb = P.inc("S")
                            GUARD["pT"] = {"S": pb}
                        else:
                            P.em("scalar", lambda s, wsl=wsl: s.activation(
                                out=pband[wsl][:, 0:512], in_=pQK[:, 0:512], func=AF.Identity,
                                bias=npi_c[:, 0:1]))
                            P.em("scalar", lambda s, wsl=wsl: s.activation(
                                out=pband[wsl][:, 512:1024], in_=pP[:, 0:512], func=AF.Identity,
                                bias=npi_c[:, 0:1]).then_inc(P.sems["S"], 1))
                            pb = P.inc("S")
                            GUARD["pQK"] = {"S": pb}
                            GUARD["pP"] = {"S": pb}
                        if snq:
                            snq.popleft()()
                        P.wait("vector", "S", pb)
                        g_wait("vector", f"ywr{wsl}")
                        for lo in (0, 512):
                            P.em("vector", lambda v, lo=lo, wsl=wsl: v.tensor_scalar(
                                out=cwr[wsl][:, lo:lo + 512], in0=pband[wsl][:, lo:lo + 512],
                                scalar1=-PI, scalar2=TWO_PI, op0=ALU.is_lt, op1=ALU.mult))
                        for lo in (0, 512):
                            P.em("vector", lambda v, lo=lo, wsl=wsl: v.tensor_add(
                                out=ywr[wsl][:, lo:lo + 512], in0=pband[wsl][:, lo:lo + 512],
                                in1=cwr[wsl][:, lo:lo + 512]))
                        for lo in (0, 512):
                            P.em("vector", lambda v, lo=lo, wsl=wsl: v.tensor_scalar(
                                out=cwr[wsl][:, lo:lo + 512], in0=ywr[wsl][:, lo:lo + 512],
                                scalar1=-HALF_PI, scalar2=TWO_PI, op0=ALU.is_lt, op1=ALU.mult))
                        for lo in (0, 512):
                            def waz(v, lo=lo, wsl=wsl, last=(lo == 512)):
                                r3 = v.tensor_add(out=yar[wsl][:, lo:lo + 512], in0=ywr[wsl][:, lo:lo + 512],
                                                  in1=cwr[wsl][:, lo:lo + 512])
                                if last:
                                    r3.then_inc(P.sems["V"], 1)
                            P.em("vector", waz)
                        wrp = P.inc("V")
                        GUARD[f"pband{wsl}"] = {"V": wrp}

                        def flush_sin(gr=gr, ksl=ksl, wsl=wsl, wrp=wrp):
                            P.wait("scalar", "V", wrp)
                            if gr >= 4:
                                P.wait("scalar", "T", kv_round[gr - 4])
                            def snS(s, ksl=ksl, wsl=wsl):
                                s.activation(out=bass.AP(kpA, ksl * 8 * 256 + 128,
                                                         [[H * 512, 128], [256, 8], [1, 128]]),
                                             in_=ywr[wsl][:, 0:1024], func=AF.Sin)
                            def snC(s, ksl=ksl, wsl=wsl):
                                s.activation(out=bass.AP(kpA, ksl * 8 * 256,
                                                         [[H * 512, 128], [256, 8], [1, 128]]),
                                             in_=yar[wsl][:, 0:1024], func=AF.Sin,
                                             bias=nph_c[:, 0:1]).then_inc(P.sems["S"], 1)
                            P.em("scalar", snS)
                            P.em("scalar", snC)
                            sin_done[gr] = P.inc("S")
                            GUARD[f"ywr{wsl}"] = {"S": sin_done[gr]}
                        snq.append(flush_sin)

                        def flush_kv(v_done=v_done, gr=gr, r=r, ti=ti, g=g,
                                     ksl=ksl, vsl=vsl, b=b):
                            P.wait("tensor", "S", sin_done[gr])
                            P.wait("tensor", "V", v_done)
                            if gr == 0:
                                g_wait("tensor", "pKVa")
                                g_wait("tensor", "pKVb")
                            if b == 1 and gr == 0:
                                P.wait("tensor", "S", kv_evac["S"])
                            for hh in range(8):
                                h = r * 8 + hh
                                first = (gr < 2 and (h % 4) == 0)
                                lastt = (ti == GT - 1 and g == NG - 1 and (h % 4) == 3)
                                def mmKV1(t, h=h, hh=hh, ksl=ksl, vsl=vsl, first=first):
                                    t.matmul(out=pKV[:, h * 128: h * 128 + 64],
                                             lhsT=kpA[:, ksl * 2048 + hh * 256: ksl * 2048 + hh * 256 + 128],
                                             rhs=vA[vsl][:, h * 64:(h + 1) * 64],
                                             start=first, stop=False)
                                def mmKV2(t, h=h, hh=hh, ksl=ksl, vsl=vsl, lastt=lastt, last=(hh == 7)):
                                    r2 = t.matmul(out=pKV[:, h * 128 + 64: h * 128 + 128],
                                                  lhsT=kpA[:, ksl * 2048 + hh * 256 + 128: ksl * 2048 + hh * 256 + 256],
                                                  rhs=vA[vsl][:, h * 64:(h + 1) * 64],
                                                  start=False, stop=lastt)
                                    if last:
                                        r2.then_inc(P.sems["T"], 1)
                                P.em("tensor", mmKV1)
                                P.em("tensor", mmKV2)
                            kv_round[gr] = P.inc("T")
                            GUARD[f"vA{vsl}"] = {"T": kv_round[gr]}
                        kvq.append(flush_kv)
                while snq:
                    snq.popleft()()
                GUARD["kqB"] = {"T": P.cnt["T"]}
                if stat_it is not None:
                    for _ in range(GT):
                        j = next(stat_it, None)
                        if j is not None:
                            emit_stat_tile(j, xres, "R%d")
            while kvq:
                kvq.popleft()()
            P.wait("scalar", "T", P.cnt["T"])
            g_wait("scalar", "kv_s")
            for q in range(4):
                def evKV(s, q=q, last=(q == 3)):
                    r = s.activation(out=kv_s[:, q * 512:(q + 1) * 512],
                                     in_=pKV[:, q * 512:(q + 1) * 512],
                                     func=AF.Identity, scale=c128_c[:, 0:1])
                    if last:
                        r.then_inc(P.sems["S"], 1)
                P.em("scalar", evKV)
            kv_evac["S"] = P.inc("S")

        store_par = {0: 0, 1: 0}

        def emit_D(b, skip_wB=False):
            GUARD["pKVa"] = {"S": kv_evac["S"]}
            GUARD["pKVb"] = {"S": kv_evac["S"]}
            if not skip_wB:
                emit_wload("B", wo_h)
            wlB = wload_cnt["B"]
            import collections
            apq = collections.deque()
            snqD = collections.deque()
            pend = [None]
            last_ffn = [0]

            def emit_subtile(g0_i0, st_i, rnd_c, apev, gp):
                i = g0_i0 + st_i
                par = st_i % 2
                g_wait("sync", f"xres{par}")
                P.em("sync", lambda e, i=i, par=par: e.dma_start(
                    out=xres[par][:, :], in_=x_h[i * 128:(i + 1) * 128, :]).then_inc(P.sems["R%d" % par], 16))
                xrc = P.inc("R%d" % par, 16)
                P.wait("tensor", "S", apev)
                P.wait("tensor", "W", wlB)
                g_wait("tensor", "pT")
                for jc2 in range(2):
                    off = jc2 * 512
                    for cc in range(8):
                        def mmW(t, st_i=st_i, jc2=jc2, cc=cc, off=off, gp=gp, last=(jc2 == 1 and cc == 7)):
                            r2 = t.matmul(out=pT[:, off:off + 512],
                                          lhsT=attnT[gp][:, cc * 512 + st_i * 128: cc * 512 + st_i * 128 + 128],
                                          rhs=wB_s[:, cc * D + jc2 * 512: cc * D + jc2 * 512 + 512],
                                          start=(cc == 0), stop=(cc == 7))
                            if last:
                                r2.then_inc(P.sems["T"], 1)
                        P.em("tensor", mmW)
                tcw = P.inc("T")
                P.wait("vector", "T", tcw)
                g_wait("vector", "woS")
                for jc2 in range(2):
                    def evW(v, lo=jc2 * 512, last=(jc2 == 1)):
                        r2 = v.tensor_add(out=woS[:, lo:lo + 512], in0=pT[:, lo:lo + 512],
                                          in1=cb_b[:, lo:lo + 512])
                        if last:
                            r2.then_inc(P.sems["V"], 1)
                    P.em("vector", evW)
                wv_c = P.inc("V")
                P.wait("tensor", "V", rnd_c)
                P.wait("tensor", "V", wv_c)
                for jc2 in range(2):
                    off = jc2 * 512
                    for dc in range(8):
                        def mmF(t, st_i=st_i, jc2=jc2, dc=dc, off=off, last=(jc2 == 1 and dc == 7)):
                            r2 = t.matmul(out=pT[:, off:off + 512],
                                          lhsT=xi8B[:, dc * 512 + st_i * 128: dc * 512 + st_i * 128 + 128],
                                          rhs=wf_s[:, dc * D + jc2 * 512: dc * D + jc2 * 512 + 512],
                                          start=(dc == 0), stop=(dc == 7))
                            if last:
                                r2.then_inc(P.sems["T"], 1)
                        P.em("tensor", mmF)
                tcf = P.inc("T")
                P.wait("vector", "T", tcf)
                P.wait("vector", "R%d" % par, xrc)
                P.wait("vector", "O%d" % par, store_par[par])
                g_wait("vector", f"xout{par}")
                for jc2 in range(2):
                    P.em("vector", lambda v, par=par, lo=jc2 * 512: v.tensor_mul(
                        out=xout[par][:, lo:lo + 512], in0=pT[:, lo:lo + 512], in1=swsc_b[:, lo:lo + 512]))
                for jc2 in range(2):
                    P.em("vector", lambda v, par=par, lo=jc2 * 512: v.tensor_add(
                        out=xout[par][:, lo:lo + 512], in0=woS[:, lo:lo + 512], in1=xout[par][:, lo:lo + 512]))
                ep_half = {}
                for jc2 in range(2):
                    def ep3(v, par=par, lo=jc2 * 512):
                        v.tensor_add(out=xout[par][:, lo:lo + 512], in0=xres[par][:, lo:lo + 512],
                                     in1=xout[par][:, lo:lo + 512]).then_inc(P.sems["V"], 1)
                    P.em("vector", ep3)
                    ep_half[jc2] = P.inc("V")
                GUARD["pT"] = {"V": ep_half[0]}
                GUARD["woS"] = {"V": ep_half[1]}
                GUARD[f"xres{par}"] = {"V": ep_half[1]}
                P.wait("sync", "V", ep_half[1])
                P.em("sync", lambda e, i=i, par=par: e.dma_start(
                    out=y_h[i * 128:(i + 1) * 128, :], in_=xout[par][:, :]).then_inc(P.sems["O%d" % par], 16))
                store_par[par] = P.inc("O%d" % par, 16)
                return tcf

            for g in range(NG):
                i0 = b * NT + g * GT
                xd = None
                for ti in range(GT):
                    xd = emit_xn_tile(i0 + ti, ti)
                emit_kq_group(wA_s, bq_s, xd, wl=wload_cnt["A"])
                if g == NG - 1 and b == 0:
                    emit_wload("A", wk_h)
                if g + 1 < NG:
                    emit_x_load(i0 + GT)
                    emit_x_load(i0 + GT + 1)
                qpm_ev = {}
                qsin_cnt = {}
                ap_nop = {}
                ap_ev = {}
                for c in range(8):
                    reg = (c % 2) * 1024
                    rname = "pKVa" if c % 2 == 0 else "pKVb"
                    par = c % 4
                    g_wait("tensor", rname)
                    if c >= 2:
                        P.wait("tensor", "S", qpm_ev[c - 2])
                    for hh in range(2):
                        h = 2 * c + hh
                        def mmQ(t, h=h, reg=reg, hh=hh, last=(hh == 1)):
                            r2 = t.matmul(out=pKV[:, reg + hh * 512: reg + hh * 512 + 512],
                                          lhsT=proj_z[h % 2][:, :],
                                          rhs=kqB[:, (h // 2) * 512:(h // 2) * 512 + 512],
                                          start=True, stop=True)
                            if last:
                                r2.then_inc(P.sems["T"], 1)
                        P.em("tensor", mmQ)
                    tc = P.inc("T")
                    if len(apq) >= 3:
                        apq.popleft()()
                    P.wait("scalar", "T", tc)
                    g_wait("scalar", f"pband{par}")
                    P.em("scalar", lambda s, par=par, reg=reg: s.activation(
                        out=pband[par][:, :], in_=pKV[:, reg: reg + 1024],
                        func=AF.Identity, bias=npi_c[:, 0:1]).then_inc(P.sems["S"], 1))
                    qpm_ev[c] = P.inc("S")
                    GUARD[rname] = {"S": qpm_ev[c]}
                    if snqD:
                        snqD.popleft()()
                    P.wait("vector", "S", qpm_ev[c])
                    g_wait("vector", f"ywr{par}")
                    for lo in (0, 512):
                        P.em("vector", lambda v, par=par, lo=lo: v.tensor_scalar(
                            out=cwr[par][:, lo:lo + 512], in0=pband[par][:, lo:lo + 512],
                            scalar1=-PI, scalar2=TWO_PI, op0=ALU.is_lt, op1=ALU.mult))
                    for lo in (0, 512):
                        P.em("vector", lambda v, par=par, lo=lo: v.tensor_add(
                            out=ywr[par][:, lo:lo + 512], in0=pband[par][:, lo:lo + 512],
                            in1=cwr[par][:, lo:lo + 512]))
                    for lo in (0, 512):
                        P.em("vector", lambda v, par=par, lo=lo: v.tensor_scalar(
                            out=cwr[par][:, lo:lo + 512], in0=ywr[par][:, lo:lo + 512],
                            scalar1=-HALF_PI, scalar2=TWO_PI, op0=ALU.is_lt, op1=ALU.mult))
                    for lo in (0, 512):
                        def waz(v, par=par, lo=lo, last=(lo == 512)):
                            r3 = v.tensor_add(out=yar[par][:, lo:lo + 512], in0=ywr[par][:, lo:lo + 512],
                                              in1=cwr[par][:, lo:lo + 512])
                            if last:
                                r3.then_inc(P.sems["V"], 1)
                        P.em("vector", waz)
                    wrp = P.inc("V")
                    GUARD[f"pband{par}"] = {"V": wrp}
                    qsl = c % 3

                    def flush_qsin(c=c, par=par, qsl=qsl, wrp=wrp):
                        P.wait("scalar", "V", wrp)
                        if c >= 3:
                            P.wait("scalar", "T", ap_nop[c - 3])
                        else:
                            g_wait("scalar", f"qp{qsl}")
                        P.em("scalar", lambda s, par=par, qsl=qsl: s.activation(
                            out=qp_sin[qsl][:, :], in_=ywr[par][:, 0:1024], func=AF.Sin))
                        P.em("scalar", lambda s, par=par, qsl=qsl: s.activation(
                            out=qp_cos[qsl][:, :], in_=yar[par][:, 0:1024], func=AF.Sin,
                            bias=nph_c[:, 0:1]).then_inc(P.sems["S"], 1))
                        qsin_cnt[c] = P.inc("S")
                        GUARD[f"ywr{par}"] = {"S": qsin_cnt[c]}
                    snqD.append(flush_qsin)
                    def flush_ap(c=c, qsl=qsl, gp=g % 2):
                        abank = pQK if c % 2 == 0 else pP
                        abname = "pQK" if c % 2 == 0 else "pP"
                        P.wait("tensor", "S", qsin_cnt[c])
                        if c >= 2:
                            P.wait("tensor", "S", ap_ev[c - 2])
                        else:
                            g_wait("tensor", abname)
                        for hh in range(2):
                            h = 2 * c + hh
                            tp = None if hh == 0 else (0, 64)
                            po = 0 if hh == 0 else 64
                            def mmA1(t, h=h, hh=hh, qsl=qsl, abank=abank, tp=tp, po=po):
                                t.matmul(out=abank[po:po + 64, 0:512],
                                         lhsT=kv_s[:, h * 128: h * 128 + 64],
                                         rhs=qp_cos[qsl][:, hh * 512: hh * 512 + 512],
                                         start=True, stop=False, tile_position=tp)
                            def mmA2(t, h=h, hh=hh, qsl=qsl, abank=abank, tp=tp, po=po):
                                r2 = t.matmul(out=abank[po:po + 64, 0:512],
                                              lhsT=kv_s[:, h * 128 + 64: h * 128 + 128],
                                              rhs=qp_sin[qsl][:, hh * 512: hh * 512 + 512],
                                              start=False, stop=True, tile_position=tp)
                                if hh == 1:
                                    r2.then_inc(P.sems["T"], 1)
                            P.em("tensor", mmA1)
                            P.em("tensor", mmA2)
                        ap_nop[c] = P.inc("T")
                        P.wait("scalar", "T", ap_nop[c])
                        g_wait("scalar", f"attnT{gp}")
                        def evA(s, c=c, abank=abank, gp=gp):
                            s.activation(out=attnT[gp][:, c * 512:(c + 1) * 512],
                                         in_=abank[:, 0:512], func=AF.Copy).then_inc(P.sems["S"], 1)
                        P.em("scalar", evA)
                        ap_ev[c] = P.inc("S")
                        GUARD[abname] = {"S": ap_ev[c]}
                    apq.append(flush_ap)
                    if pend[0] is not None and c % 2 == 1:
                        pg_i0, pg_rnd, pg_apev, pg_gp = pend[0]
                        last_ffn[0] = emit_subtile(pg_i0, c // 2, pg_rnd, pg_apev, pg_gp)
                        if c == 7:
                            GUARD[f"attnT{pg_gp}"] = {"T": last_ffn[0]}
                            pend[0] = None
                while snqD:
                    snqD.popleft()()
                while apq:
                    apq.popleft()()
                for _j in range(3):
                    GUARD[f"qp{_j}"] = {"T": P.cnt["T"]}
                g_wait("vector", "xi8B")
                P.wait("vector", "T", last_ffn[0])
                for dc in range(8):
                    P.em("vector", lambda v, dc=dc: v.tensor_scalar(
                        out=rtmp[:, 0:512],
                        in0=xnB[:, dc * 512:(dc + 1) * 512],
                        scalar1=invs_c[:, 0:1], scalar2=MAGIC, op0=ALU.mult, op1=ALU.add))
                    def rnd2(v, dc=dc, last=(dc == 7)):
                        r = v.tensor_scalar(out=xi8B[:, dc * 512:(dc + 1) * 512],
                                            in0=rtmp[:, 0:512],
                                            scalar1=MAGIC, scalar2=None, op0=ALU.subtract)
                        if last:
                            r.then_inc(P.sems["V"], 1)
                    P.em("vector", rnd2)
                rnd_cnt = P.inc("V")
                GUARD["xnB_sc"] = {"V": rnd_cnt}
                GUARD["xnB_v"] = {"V": rnd_cnt}
                pend[0] = (i0, rnd_cnt, ap_ev[7], g % 2)
            pg_i0, pg_rnd, pg_apev, pg_gp = pend[0]
            for st_i in range(GT):
                last_ffn[0] = emit_subtile(pg_i0, st_i, pg_rnd, pg_apev, pg_gp)
            pend[0] = None
            GUARD[f"attnT{pg_gp}"] = {"T": last_ffn[0]}
            GUARD["xi8B"] = {"T": last_ffn[0]}

        emit_C(0, stat_it=iter(range(NT, NTILES)))
        GUARD["xout0"] = {"S": a_sc[NTILES - 2]}
        GUARD["xout1"] = {"S": a_sc[NTILES - 1]}
        emit_wload("B", wo_h)
        emit_B1(NT, NTILES)
        emit_B2()
        emit_D(0, skip_wB=True)
        emit_C(1)
        emit_D(1)

        P.wait("sync", "O0", P.cnt["O0"])
        P.wait("sync", "O1", P.cnt["O1"])

        block = S.enter_context(nc.Block())

        @block.sync
        def _(sync):
            for f in P.q["sync"]:
                f(sync)

        @block.vector
        def _(vector):
            for f in P.q["vector"]:
                f(vector)

        @block.scalar
        def _(scalar):
            for f in P.q["scalar"]:
                f(scalar)

        @block.tensor
        def _(tensor):
            for f in P.q["tensor"]:
                f(tensor)

        @block.gpsimd
        def _(gpsimd):
            for f in P.q["gpsimd"]:
                f(gpsimd)

    return nc


def _prep_w(w):
    return np.ascontiguousarray(w.reshape(8, 128, 1024).transpose(1, 0, 2).reshape(128, 8192))


_NC_CACHE = {}


def kernel(L0_attn_x, L0_attn_proj_mat, L0_attn_Wq, L0_attn_bq, L0_attn_Wk, L0_attn_bk,
           L0_attn_Wv, L0_attn_bv, L0_attn_Wo, L0_attn_bo,
           L0_ffn_weight_int8, L0_ffn_weight_scale, L0_ffn_bias):
    x = np.ascontiguousarray(np.asarray(L0_attn_x, dtype=np.float32))
    wq = _prep_w(np.ascontiguousarray(np.asarray(L0_attn_Wq, dtype=np.float32).T))
    wk = _prep_w(np.ascontiguousarray(np.asarray(L0_attn_Wk, dtype=np.float32).T))
    wv = _prep_w(np.ascontiguousarray(np.asarray(L0_attn_Wv, dtype=np.float32).T))
    wo = _prep_w(np.ascontiguousarray(np.asarray(L0_attn_Wo, dtype=np.float32).T))
    wf = _prep_w(np.ascontiguousarray(np.asarray(L0_ffn_weight_int8).T))
    proj = np.ascontiguousarray(np.vstack([np.asarray(L0_attn_proj_mat, dtype=np.float32)] * 2))
    bq = np.ascontiguousarray(np.asarray(L0_attn_bq, dtype=np.float32).reshape(8, 128).T)
    bk = np.ascontiguousarray(np.asarray(L0_attn_bk, dtype=np.float32).reshape(8, 128).T)
    bv = np.asarray(L0_attn_bv, dtype=np.float32).reshape(1, D)
    bo = np.asarray(L0_attn_bo, dtype=np.float32).reshape(1, D)
    fb = np.asarray(L0_ffn_bias, dtype=np.float32).reshape(1, D)
    wsc = np.ascontiguousarray(np.asarray(L0_ffn_weight_scale, dtype=np.float32).reshape(1, D))
    ident = np.eye(128, dtype=np.float32)

    if "nc" not in _NC_CACHE:
        _NC_CACHE["nc"] = build_nc()
    nc = _NC_CACHE["nc"]

    in_maps = []
    for c in range(8):
        xs = np.ascontiguousarray(x[2 * c:2 * c + 2].reshape(NB * TOK, D))
        in_maps.append({
            "x": xs, "wq": wq, "wk": wk, "wv": wv, "wo": wo, "wf": wf,
            "proj": proj, "bq": bq, "bk": bk, "bv": bv, "bo": bo,
            "fb": fb, "wsc": wsc, "id_h": ident,
        })
    res = run_bass_kernel_spmd(nc, in_maps, core_ids=list(range(8)))
    out = np.empty((16, TOK, D), dtype=np.float32)
    for c in range(8):
        out[2 * c:2 * c + 2] = res.results[c]["y"].reshape(2, TOK, D)
    return out


# version 31
# speedup vs baseline: 1.0103x; 1.0103x over previous
import contextlib
import numpy as np

import concourse.bass as bass
import concourse.mybir as mybir
from concourse.bass_utils import run_bass_kernel_spmd

F32 = mybir.dt.float32
BF16 = mybir.dt.bfloat16
I8 = mybir.dt.int8
AF = mybir.ActivationFunctionType
ALU = mybir.AluOpType
AX = mybir.AxisListType.X

PI = float(np.pi)
TWO_PI = float(2.0 * np.pi)
HALF_PI = float(np.pi / 2.0)
MAGIC = 12582912.0
EPS_LN = 1e-5
EPS_Q = 1e-5

D = 1024
H = 16
NB = 2
TOK = 4096
NT = 32
NG = 8
GT = 4
NTILES = NB * NT

ENGINES = ("sync", "vector", "scalar", "tensor", "gpsimd")
_REPLICA_GROUPS = [list(range(8))]
SEMTAGS = ("X0", "X1", "R0", "R1", "O0", "O1", "W", "T", "S", "V", "C", "G")


class Prog:
    def __init__(self):
        self.q = {e: [] for e in ENGINES}
        self.cnt = {t: 0 for t in SEMTAGS}
        self.sems = {}
        self.meta = {}
        self.last_inc_eng = {}
        self.cur_eng = None

    def em(self, eng, fn, meta=None):
        self.q[eng].append(fn)
        self.meta.setdefault(eng, []).append(meta)

    def inc(self, tag, n=1):
        self.cnt[tag] += n
        self.last_inc_eng[tag] = self.cur_eng
        return self.cnt[tag]

    def wait(self, eng, tag, upto):
        if upto > 0:
            s = self.sems[tag]
            self.em(eng, lambda e, s=s, v=upto: e.wait_ge(s, v), meta=("wait", tag, upto))

    def waits(self, eng, guard):
        for tag, v in guard.items():
            self.wait(eng, tag, v)


def build_nc():
    nc = bass.Bass("TRN2", target_bir_lowering=False, dynamic_dma_scratch_size=8192,
                   detect_race_conditions=False)

    x_h = nc.dram_tensor("x", [NB * TOK, D], F32, kind="ExternalInput")
    wq_h = nc.dram_tensor("wq", [128, 8 * D], F32, kind="ExternalInput")
    wk_h = nc.dram_tensor("wk", [128, 8 * D], F32, kind="ExternalInput")
    wv_h = nc.dram_tensor("wv", [128, 8 * D], F32, kind="ExternalInput")
    wo_h = nc.dram_tensor("wo", [128, 8 * D], F32, kind="ExternalInput")
    wf_h = nc.dram_tensor("wf", [128, 8 * D], I8, kind="ExternalInput")
    proj_h = nc.dram_tensor("proj", [128, 128], F32, kind="ExternalInput")
    bq_h = nc.dram_tensor("bq", [128, 8], F32, kind="ExternalInput")
    bk_h = nc.dram_tensor("bk", [128, 8], F32, kind="ExternalInput")
    bv_h = nc.dram_tensor("bv", [1, D], F32, kind="ExternalInput")
    bo_h = nc.dram_tensor("bo", [1, D], F32, kind="ExternalInput")
    fb_h = nc.dram_tensor("fb", [1, D], F32, kind="ExternalInput")
    wsc_h = nc.dram_tensor("wsc", [1, D], F32, kind="ExternalInput")
    id_h = nc.dram_tensor("id_h", [128, 128], F32, kind="ExternalInput")
    y_h = nc.dram_tensor("y", [NB * TOK, D], F32, kind="ExternalOutput")
    cc_in = nc.dram_tensor("cc_in", [1, 8], F32)
    cc_out = nc.dram_tensor("cc_out", [1, 8], F32)

    S = contextlib.ExitStack()
    sb = lambda name, shape, dt: S.enter_context(nc.sbuf_tensor(name, shape, dt))
    psum = lambda name, shape: S.enter_context(nc.psum_tensor(name, shape, F32))

    with S:
        wA_s = sb("wA_s", [128, 8 * D], BF16)
        wB_s = sb("wB_s", [128, 8 * D], BF16)
        wf_i = sb("wf_i", [128, D], I8)
        wf_s = sb("wf_s", [128, 8 * D], BF16)
        proj_f = sb("proj_f", [128, 128], F32)
        proj_b = sb("proj_b", [128, 128], BF16)
        proj_z = [sb("proj_z0", [128, 128], BF16), sb("proj_z1", [128, 128], BF16)]
        maskh = [sb("maskh0", [128, 1], F32), sb("maskh1", [128, 1], F32)]
        id_f = sb("id_f", [128, 128], F32)
        id_b = sb("id_b", [128, 128], BF16)
        bq_s = sb("bq_s", [128, 8], F32)
        bk_s = sb("bk_s", [128, 8], F32)
        bias_pack = sb("bias_pack", [65, 2 * D], F32)
        ones_r = sb("ones_r", [65, 128], F32)
        nph_c = sb("nph_c", [128, 1], F32)
        npi_c = sb("npi_c", [128, 1], F32)
        c128_c = sb("c128_c", [128, 1], F32)
        eighth_c = sb("eighth_c", [128, 1], F32)
        st_sum = sb("st_sum", [128, NTILES], F32)
        st_ssq = sb("st_ssq", [128, NTILES], F32)
        st_max = sb("st_max", [128, NTILES], F32)
        st_min = sb("st_min", [128, NTILES], F32)
        negm = sb("negm", [128, NTILES], F32)
        rstd = sb("rstd", [128, NTILES], F32)
        negmr = sb("negmr", [128, NTILES], F32)
        tmp64a = sb("tmp64a", [128, NTILES], F32)
        tmp64b = sb("tmp64b", [128, NTILES], F32)
        maxcol = sb("maxcol", [128, 1], F32)
        mrow = sb("mrow", [1, 128], F32)
        lmax = sb("lmax", [1, 1], F32)
        s_sc = sb("s_sc", [1, 1], F32)
        invs_sc = sb("invs_sc", [1, 1], F32)
        invs_c = sb("invs_c", [128, 1], F32)
        gs8 = sb("gs8", [1, 8], F32)
        sg8 = sb("sg8", [1, 8], F32)
        swsc_b = sb("swsc_b", [128, D], F32)
        bvb_b = sb("bvb_b", [128, D], BF16)
        cb_b = sb("cb_b", [128, D], BF16)
        xin = [sb("xin0", [128, D], F32), sb("xin1", [128, D], F32)]
        xres = [sb("xres0", [128, D], F32), sb("xres1", [128, D], F32)]
        xnbf = [sb("xnbf0", [128, D], BF16), sb("xnbf1", [128, D], BF16)]
        xnB = sb("xnB", [128, 8 * 512], BF16)
        kqB = sb("kqB", [128, 8 * 512], BF16)
        kpA = sb("kpA", [128, H * 512], BF16)
        vA = [sb("vA0", [128, D], BF16), sb("vA1", [128, D], BF16),
              sb("vA2", [128, D], BF16), sb("vA3", [128, D], BF16)]
        pband = [sb(f"pband{j}", [128, 1024], BF16) for j in range(4)]
        ywr = [sb(f"ywr{j}", [128, 1024], BF16) for j in range(4)]
        cwr = [sb(f"cwr{j}", [128, 1024], BF16) for j in range(4)]
        yar = [sb(f"yar{j}", [128, 1024], BF16) for j in range(4)]
        kv_s = sb("kv_s", [128, H * 128], BF16)
        qp_cos = [sb(f"qpc{j}", [128, 1024], BF16) for j in range(3)]
        qp_sin = [sb(f"qps{j}", [128, 1024], BF16) for j in range(3)]
        attnT = [sb("attnT0", [128, 8 * 512], BF16), sb("attnT1", [128, 8 * 512], BF16)]
        xi8B = sb("xi8B", [128, 8 * 512], BF16)
        rtmp = sb("rtmp", [128, 512], F32)
        woS = sb("woS", [128, 1024], BF16)
        xout = [sb("xout0", [128, D], F32), sb("xout1", [128, D], F32)]

        pT = psum("pT", [128, 1024])
        pQK = psum("pQK", [128, 512])
        pP = psum("pP", [128, 512])
        pKV = psum("pKV", [128, 2048])

        P = Prog()
        P.sems = {t: S.enter_context(nc.semaphore("s" + t)) for t in SEMTAGS}

        GUARD = {}

        def g_wait(eng, name):
            P.waits(eng, GUARD.get(name, {}))

        def g_read(name, tag, cnt):
            GUARD.setdefault(name, {})[tag] = cnt

        for src, dst in ((wk_h, wA_s[:, :]), (wv_h, wB_s[:, :]),
                         (proj_h, proj_f[:, :]), (id_h, id_f[:, :]),
                         (bq_h, bq_s[:, :]), (bk_h, bk_s[:, :]),
                         (bv_h, bias_pack[64:65, 0:D]), (bo_h, bias_pack[32:33, D:2 * D]),
                         (fb_h, bias_pack[32:33, 0:D]), (wsc_h, bias_pack[0:1, 0:D])):
            P.em("gpsimd", lambda g, a=src, b=dst: g.dma_start(out=b, in_=a[:, :]).then_inc(P.sems["W"], 16))
            P.inc("W", 16)
        wfv = 0
        for ch in range(8):
            if ch >= 1:
                P.wait("gpsimd", "V", wfv)
            P.em("gpsimd", lambda g, ch=ch: g.dma_start(
                out=wf_i[:, :], in_=wf_h[:, ch * D:(ch + 1) * D]).then_inc(P.sems["W"], 16))
            wc = P.inc("W", 16)
            P.wait("vector", "W", wc)
            P.em("vector", lambda v, ch=ch: v.tensor_copy(
                out=wf_s[:, ch * D:(ch + 1) * D], in_=wf_i[:, :]).then_inc(P.sems["V"], 1))
            wfv = P.inc("V")
        w0 = P.cnt["W"]

        P.em("vector", lambda v: v.memset(ones_r[:], 1.0))
        P.em("vector", lambda v: v.memset(nph_c[:], -HALF_PI))
        P.em("vector", lambda v: v.memset(npi_c[:], -PI))
        P.em("vector", lambda v: v.memset(c128_c[:], 1.0 / 128.0))
        P.em("vector", lambda v: v.memset(eighth_c[:], 0.125))
        P.wait("vector", "W", w0)
        P.em("vector", lambda v: v.tensor_copy(out=id_b[:], in_=id_f[:]).then_inc(P.sems["V"], 1))
        v_const = P.inc("V")
        P.wait("scalar", "V", v_const)
        P.wait("scalar", "W", w0)
        P.em("scalar", lambda s: s.activation(out=proj_b[:], in_=proj_f[:], func=AF.Identity,
                                              scale=eighth_c[:, 0:1]).then_inc(P.sems["S"], 1))
        s_const = P.inc("S")
        P.em("vector", lambda v: v.memset(maskh[0][0:64, 0:1], 1.0))
        P.em("vector", lambda v: v.memset(maskh[0][64:128, 0:1], 0.0))
        P.em("vector", lambda v: v.memset(maskh[1][0:64, 0:1], 0.0))
        P.em("vector", lambda v: v.memset(maskh[1][64:128, 0:1], 1.0))
        P.wait("vector", "S", s_const)
        for zz in range(2):
            P.em("vector", lambda v, zz=zz: v.tensor_scalar(
                out=proj_z[zz][:, :], in0=proj_b[:, :], scalar1=maskh[zz][:, 0:1],
                scalar2=None, op0=ALU.mult).then_inc(P.sems["V"], 1))
            P.inc("V")

        def vstep(fn):
            P.em("vector", lambda v, fn=fn: fn(v).then_inc(P.sems["V"], 1))
            c = P.inc("V")
            P.wait("vector", "V", c)
            return c

        P.wait("vector", "W", w0)
        vstep(lambda v: v.tensor_add(out=bias_pack[32:33, 0:D], in0=bias_pack[32:33, 0:D],
                                     in1=bias_pack[32:33, D:2 * D]))
        bv0 = P.cnt["V"]
        P.wait("tensor", "V", bv0)
        P.wait("tensor", "W", w0)
        P.em("tensor", lambda t: t.matmul(out=pT[0:128, 0:512], lhsT=ones_r[64:65, 0:128],
                                          rhs=bias_pack[64:65, 0:512], start=True, stop=True))
        P.em("tensor", lambda t: t.matmul(out=pT[0:128, 512:1024], lhsT=ones_r[64:65, 0:128],
                                          rhs=bias_pack[64:65, 512:1024], start=True, stop=True))
        P.em("tensor", lambda t: t.matmul(out=pQK[0:128, 0:512], lhsT=ones_r[32:33, 0:128],
                                          rhs=bias_pack[32:33, 0:512], start=True, stop=True))
        P.em("tensor", lambda t: t.matmul(out=pP[0:128, 0:512], lhsT=ones_r[32:33, 0:128],
                                          rhs=bias_pack[32:33, 512:1024], start=True, stop=True).then_inc(P.sems["T"], 1))
        bt3 = P.inc("T")
        P.wait("vector", "T", bt3)
        vstep(lambda v: v.tensor_copy(out=bvb_b[:, :], in_=pT[:, 0:1024]))
        vstep(lambda v: v.tensor_copy(out=cb_b[:, 0:512], in_=pQK[:, 0:512]))
        bv9 = vstep(lambda v: v.tensor_copy(out=cb_b[:, 512:1024], in_=pP[:, 0:512]))
        GUARD["pT"] = {"V": bv9}
        GUARD["pQK"] = {"V": bv9}
        GUARD["pP"] = {"V": bv9}

        a_sc = [0] * NTILES
        a_v = [0] * NTILES

        def emit_stat_tile(i, buf, rtag):
            par = i % 2
            gname = ("xin%d" % par) if buf is xin else ("xres%d" % par)
            g_wait("sync", gname)
            P.em("sync", lambda e, i=i, par=par, buf=buf: e.dma_start(
                out=buf[par][:, :], in_=x_h[i * 128:(i + 1) * 128, :]).then_inc(P.sems[rtag % par], 16))
            xc = P.inc(rtag % par, 16)
            P.wait("scalar", rtag % par, xc)
            if i >= 2:
                P.wait("scalar", "S", a_sc[i - 2])
            P.em("scalar", lambda s, i=i, par=par, buf=buf: s.activation(
                out=xout[par][:, :], in_=buf[par][:, :], func=AF.Square,
                accum_out=st_ssq[:, i:i + 1]))
            P.em("scalar", lambda s, i=i, par=par, buf=buf: s.activation(
                out=xout[par][:, :], in_=buf[par][:, :], func=AF.Identity,
                accum_out=st_sum[:, i:i + 1]).then_inc(P.sems["S"], 1))
            a_sc[i] = P.inc("S")
            P.wait("vector", rtag % par, xc)
            P.em("vector", lambda v, i=i, par=par, buf=buf: v.tensor_reduce(
                out=st_max[:, i:i + 1], in_=buf[par][:, :], axis=AX, op=ALU.max).then_inc(P.sems["V"], 1))
            P.inc("V")
            P.em("vector", lambda v, i=i, par=par, buf=buf: v.tensor_reduce(
                out=st_min[:, i:i + 1], in_=buf[par][:, :], axis=AX, op=ALU.min).then_inc(P.sems["V"], 1))
            a_v[i] = P.inc("V")
            GUARD[gname] = {"S": a_sc[i], "V": a_v[i]}

        for i in range(NT):
            emit_stat_tile(i, xin, "X%d")

        def emit_B1(c0, c1):
            P.wait("vector", "S", a_sc[c1 - 1])
            P.wait("vector", "V", a_v[c1 - 1])
            sl = slice(c0, c1)
            vstep(lambda v: v.tensor_scalar(out=negm[:, sl], in0=st_sum[:, sl], scalar1=-1.0 / D,
                                            scalar2=None, op0=ALU.mult))
            vstep(lambda v: v.tensor_mul(out=tmp64a[:, sl], in0=negm[:, sl], in1=negm[:, sl]))
            vstep(lambda v: v.scalar_tensor_tensor(out=tmp64b[:, sl], in0=st_ssq[:, sl], scalar=1.0 / D,
                                                   in1=tmp64a[:, sl], op0=ALU.mult, op1=ALU.subtract))
            bv1 = vstep(lambda v: v.tensor_scalar(out=tmp64b[:, sl], in0=tmp64b[:, sl], scalar1=EPS_LN,
                                                  scalar2=None, op0=ALU.add))
            P.wait("scalar", "V", bv1)
            P.em("scalar", lambda s, sl=sl: s.activation(out=tmp64a[:, sl], in_=tmp64b[:, sl],
                                                         func=AF.Sqrt).then_inc(P.sems["S"], 1))
            bs1 = P.inc("S")
            P.wait("vector", "S", bs1)
            vstep(lambda v: v.reciprocal(out=rstd[:, sl], in_=tmp64a[:, sl]))
            vstep(lambda v: v.tensor_mul(out=negmr[:, sl], in0=negm[:, sl], in1=rstd[:, sl]))
            vstep(lambda v: v.tensor_add(out=tmp64a[:, sl], in0=st_max[:, sl], in1=negm[:, sl]))
            vstep(lambda v: v.tensor_add(out=tmp64b[:, sl], in0=st_min[:, sl], in1=negm[:, sl]))
            vstep(lambda v: v.tensor_scalar(out=tmp64b[:, sl], in0=tmp64b[:, sl], scalar1=-1.0,
                                            scalar2=None, op0=ALU.mult))
            vstep(lambda v: v.tensor_max(out=tmp64a[:, sl], in0=tmp64a[:, sl], in1=tmp64b[:, sl]))
            return vstep(lambda v: v.tensor_mul(out=tmp64a[:, sl], in0=tmp64a[:, sl], in1=rstd[:, sl]))

        b1v = emit_B1(0, NT)
        P.wait("scalar", "V", b1v)

        def emit_B2():
            bv2 = vstep(lambda v: v.tensor_reduce(out=maxcol[:, 0:1], in_=tmp64a[:, 0:NTILES],
                                                  axis=AX, op=ALU.max))
            P.wait("tensor", "V", bv2)
            g_wait("tensor", "pT")
            P.em("tensor", lambda t: t.matmul(out=pT[0:1, 0:128], lhsT=maxcol[:, 0:1], rhs=id_f[:, :],
                                              start=True, stop=True).then_inc(P.sems["T"], 1))
            bt1 = P.inc("T")
            P.wait("vector", "T", bt1)
            vstep(lambda v: v.tensor_copy(out=mrow[:], in_=pT[0:1, 0:128]))
            vstep(lambda v: v.tensor_reduce(out=lmax[0:1, 0:1], in_=mrow[0:1, 0:128], axis=AX, op=ALU.max))
            bv4 = vstep(lambda v: v.tensor_scalar(out=gs8[0:1, 0:8], in0=ones_r[0:1, 0:8],
                                                  scalar1=lmax[0:1, 0:1], scalar2=None, op0=ALU.mult))
            P.wait("gpsimd", "V", bv4)
            P.em("gpsimd", lambda g: g.dma_start(out=cc_in[:, :], in_=gs8[:, :]).then_inc(P.sems["W"], 16))
            w1 = P.inc("W", 16)
            P.wait("gpsimd", "W", w1)
            P.em("gpsimd", lambda g: g.collective_compute(
                "AllReduce", ALU.max, replica_groups=_REPLICA_GROUPS,
                ins=[cc_in.ap().opt()], outs=[cc_out.ap().opt()]).then_inc(P.sems["C"], 1))
            cc1 = P.inc("C")
            P.wait("gpsimd", "C", cc1)
            P.em("gpsimd", lambda g: g.dma_start(out=sg8[:, :], in_=cc_out[:, :]).then_inc(P.sems["W"], 16))
            w2 = P.inc("W", 16)
            P.wait("vector", "W", w2)
            vstep(lambda v: v.tensor_scalar(out=s_sc[0:1, 0:1], in0=sg8[0:1, 0:1],
                                            scalar1=1.0 / 127.0, scalar2=EPS_Q,
                                            op0=ALU.mult, op1=ALU.max))
            vstep(lambda v: v.reciprocal(out=invs_sc[0:1, 0:1], in_=s_sc[0:1, 0:1]))
            vstep(lambda v: v.tensor_scalar(out=bias_pack[0:1, D:2 * D], in0=bias_pack[0:1, 0:D],
                                            scalar1=s_sc[0:1, 0:1], scalar2=None, op0=ALU.mult))
            bv8 = P.cnt["V"]
            P.wait("tensor", "V", bv8)
            g_wait("tensor", "pQK")
            P.em("tensor", lambda t: t.matmul(out=pQK[0:128, 0:1], lhsT=ones_r[0:1, 0:128],
                                              rhs=invs_sc[0:1, 0:1], start=True, stop=True))
            P.em("tensor", lambda t: t.matmul(out=pT[0:128, 0:512], lhsT=ones_r[0:1, 0:128],
                                              rhs=bias_pack[0:1, D:D + 512], start=True, stop=True))
            P.em("tensor", lambda t: t.matmul(out=pT[0:128, 512:1024], lhsT=ones_r[0:1, 0:128],
                                              rhs=bias_pack[0:1, D + 512:D + 1024],
                                              start=True, stop=True).then_inc(P.sems["T"], 1))
            bt2 = P.inc("T")
            P.wait("vector", "T", bt2)
            vstep(lambda v: v.tensor_copy(out=invs_c[:, 0:1], in_=pQK[:, 0:1]))
            bv9a = vstep(lambda v: v.tensor_copy(out=swsc_b[:, :], in_=pT[:, 0:1024]))
            GUARD["pT"] = {"V": bv9a}
            GUARD["pQK"] = {"V": bv9a}
            P.wait("scalar", "V", bv9a)

        sc_xn = [0] * NTILES
        xnb_ev = {}
        xld = {}

        def emit_x_load(i):
            par = i % 2
            g_wait("sync", f"xin{par}")
            P.em("sync", lambda e, i=i, par=par: e.dma_start(
                out=xin[par][:, :], in_=x_h[i * 128:(i + 1) * 128, :]).then_inc(P.sems["X%d" % par], 16))
            xld[i] = P.inc("X%d" % par, 16)
            return xld[i]

        def emit_xn_tile(i, ti):
            par = i % 2
            xc = xld.pop(i, None)
            if xc is None:
                xc = emit_x_load(i)
                del xld[i]
            P.wait("vector", "X%d" % par, xc)
            g_wait("vector", f"xnbf{par}")
            P.em("vector", lambda v, i=i, par=par: v.tensor_scalar(
                out=xnbf[par][:, :], in0=xin[par][:, :],
                scalar1=rstd[:, i:i + 1], scalar2=negmr[:, i:i + 1],
                op0=ALU.mult, op1=ALU.add).then_inc(P.sems["V"], 1))
            sc_xn[i] = P.inc("V")
            GUARD[f"xin{par}"] = {"V": sc_xn[i]}
            P.wait("tensor", "V", sc_xn[i])
            g_wait("tensor", "pQK")
            g_wait("tensor", "pP")
            for c in range(8):
                def mmT(t, par=par, c=c, last=(c == 7)):
                    bank = pQK if c < 4 else pP
                    r = t.matmul(out=bank[:, (c % 4) * 128:(c % 4) * 128 + 128],
                                 lhsT=xnbf[par][:, c * 128:(c + 1) * 128],
                                 rhs=id_b[:, :], start=True, stop=True)
                    if last:
                        r.then_inc(P.sems["T"], 1)
                P.em("tensor", mmT)
            tc = P.inc("T")
            GUARD[f"xnbf{par}"] = {"T": tc}
            P.wait("scalar", "T", tc)
            g_wait("scalar", "xnB_sc")
            P.em("scalar", lambda s, ti=ti: s.activation(
                out=bass.AP(xnB, ti * 128, [[8 * 512, 128], [512, 4], [1, 128]]),
                in_=pQK[:, 0:512], func=AF.Copy).then_inc(P.sems["S"], 1))
            e_s = P.inc("S")
            P.wait("vector", "T", tc)
            g_wait("vector", "xnB_v")
            P.em("vector", lambda v, ti=ti: v.tensor_copy(
                out=bass.AP(xnB, 4 * 512 + ti * 128, [[8 * 512, 128], [512, 4], [1, 128]]),
                in_=pP[:, 0:512]).then_inc(P.sems["V"], 1))
            e_v = P.inc("V")
            GUARD["pQK"] = {"S": e_s}
            GUARD["pP"] = {"V": e_v}
            return e_s, e_v

        def emit_kq_group(w_s, b_s, xnb_done, wl=0):
            P.wait("tensor", "S", xnb_done[0])
            P.wait("tensor", "V", xnb_done[1])
            P.wait("tensor", "W", wl)
            ev = {}
            for jc in range(8):
                bname = "pQK" if jc % 2 == 0 else "pP"
                bank = pQK if jc % 2 == 0 else pP
                if jc >= 2:
                    P.wait("tensor", "S", ev[jc - 2])
                else:
                    g_wait("tensor", bname)
                for dc in range(8):
                    def mmK(t, jc=jc, dc=dc, bank=bank, w_s=w_s, last=(dc == 7)):
                        r = t.matmul(out=bank[:, 0:512],
                                     lhsT=w_s[:, dc * D + jc * 128: dc * D + jc * 128 + 128],
                                     rhs=xnB[:, dc * 512:(dc + 1) * 512],
                                     start=(dc == 0), stop=(dc == 7))
                        if last:
                            r.then_inc(P.sems["T"], 1)
                    P.em("tensor", mmK)
                tc = P.inc("T")
                P.wait("scalar", "T", tc)
                g_wait("scalar", "kqB")
                def evK(s, jc=jc, bank=bank, b_s=b_s):
                    s.activation(out=kqB[:, jc * 512:(jc + 1) * 512], in_=bank[:, 0:512],
                                 func=AF.Identity, bias=b_s[:, jc:jc + 1]).then_inc(P.sems["S"], 1)
                P.em("scalar", evK)
                ev[jc] = P.inc("S")
            GUARD["pQK"] = {"S": ev[6]}
            GUARD["pP"] = {"S": ev[7]}
            return ev

        wload_cnt = {"A": 0, "B": 0}

        def emit_wload(which, src_h):
            dst = wA_s if which == "A" else wB_s
            P.wait("gpsimd", "T", P.cnt["T"])
            P.em("gpsimd", lambda g, s2=src_h, d2=dst: g.dma_start(
                out=d2[:, :], in_=s2[:, :]).then_inc(P.sems["W"], 16))
            wload_cnt[which] = P.inc("W", 16)
            return wload_cnt[which]

        kv_evac = {"S": 0}

        def emit_C(b, stat_it=None):
            import collections
            kvq = collections.deque()
            snq = collections.deque()
            kv_round = {}
            sin_done = {}
            grc = [0]
            if b == 1:
                emit_wload("B", wv_h)
            wlB = wload_cnt["B"]
            def emit_V_mm(ti):
                vsl = ti % 4
                g_wait("tensor", f"vA{vsl}")
                g_wait("tensor", "pT")
                tcs = []
                for jc2 in range(2):
                    off = jc2 * 512
                    for dc in range(8):
                        def mmV(t, ti=ti, jc2=jc2, dc=dc, off=off, last=(dc == 7)):
                            r2 = t.matmul(out=pT[:, off:off + 512],
                                          lhsT=xnB[:, dc * 512 + ti * 128: dc * 512 + ti * 128 + 128],
                                          rhs=wB_s[:, dc * D + jc2 * 512: dc * D + jc2 * 512 + 512],
                                          start=(dc == 0), stop=(dc == 7))
                            if last:
                                r2.then_inc(P.sems["T"], 1)
                        P.em("tensor", mmV)
                    tcs.append(P.inc("T"))
                return tcs

            def emit_V_ev(ti, tcs):
                vsl = ti % 4
                for jc2 in range(2):
                    P.wait("vector", "T", tcs[jc2])
                    def evV(v, jc2=jc2, off=jc2 * 512, vsl=vsl):
                        v.tensor_add(out=vA[vsl][:, jc2 * 512:(jc2 + 1) * 512],
                                     in0=pT[:, off:off + 512],
                                     in1=bvb_b[:, jc2 * 512:(jc2 + 1) * 512]).then_inc(P.sems["V"], 1)
                    P.em("vector", evV)
                    P.inc("V")
                GUARD["pT"] = {"V": P.cnt["V"]}
                return P.cnt["V"]

            for g in range(NG):
                i0 = b * NT + g * GT
                xd = None
                for ti in range(GT):
                    xd = emit_xn_tile(i0 + ti, ti)
                emit_kq_group(wA_s, bk_s, xd, wl=wload_cnt["A"])
                if g == NG - 1:
                    emit_wload("A", wq_h)
                if g + 1 < NG:
                    emit_x_load(i0 + GT)
                    emit_x_load(i0 + GT + 1)
                if g == 0:
                    P.wait("tensor", "W", wlB)
                for ti in range(GT):
                    i = i0 + ti
                    vsl = ti % 4
                    v_done = emit_V_ev(ti, emit_V_mm(ti))
                    for r in range(2):
                        gr = grc[0]; grc[0] += 1
                        rb = gr % 2
                        ksl = gr % 4
                        wsl = gr % 4
                        use_pt = (rb == 0) or (gr % 8 == 7)
                        if use_pt:
                            g_wait("tensor", "pT")
                            pdst = [(pT, 0), (pT, 512)]
                        else:
                            g_wait("tensor", "pQK")
                            g_wait("tensor", "pP")
                            pdst = [(pQK, 0), (pP, 0)]
                        for hh in range(8):
                            h = r * 8 + hh
                            bank, off = pdst[hh // 4]
                            def mmP(t, h=h, hh=hh, ti=ti, bank=bank, off=off, last=(hh == 7)):
                                r2 = t.matmul(out=bank[:, off + (hh % 4) * 128: off + (hh % 4) * 128 + 128],
                                              lhsT=kqB[:, (h // 2) * 512 + ti * 128:(h // 2) * 512 + ti * 128 + 128],
                                              rhs=proj_z[h % 2][:, :],
                                              start=True, stop=True)
                                if last:
                                    r2.then_inc(P.sems["T"], 1)
                            P.em("tensor", mmP)
                        tc = P.inc("T")
                        if len(kvq) >= 5:
                            kvq.popleft()()
                        P.wait("scalar", "T", tc)
                        g_wait("scalar", f"pband{wsl}")
                        if use_pt:
                            P.em("scalar", lambda s, wsl=wsl: s.activation(
                                out=pband[wsl][:, :], in_=pT[:, 0:1024], func=AF.Identity,
                                bias=npi_c[:, 0:1]).then_inc(P.sems["S"], 1))
                            pb = P.inc("S")
                            GUARD["pT"] = {"S": pb}
                        else:
                            P.em("scalar", lambda s, wsl=wsl: s.activation(
                                out=pband[wsl][:, 0:512], in_=pQK[:, 0:512], func=AF.Identity,
                                bias=npi_c[:, 0:1]))
                            P.em("scalar", lambda s, wsl=wsl: s.activation(
                                out=pband[wsl][:, 512:1024], in_=pP[:, 0:512], func=AF.Identity,
                                bias=npi_c[:, 0:1]).then_inc(P.sems["S"], 1))
                            pb = P.inc("S")
                            GUARD["pQK"] = {"S": pb}
                            GUARD["pP"] = {"S": pb}
                        if snq:
                            snq.popleft()()
                        P.wait("vector", "S", pb)
                        g_wait("vector", f"ywr{wsl}")
                        for lo in (0, 512):
                            P.em("vector", lambda v, lo=lo, wsl=wsl: v.tensor_scalar(
                                out=cwr[wsl][:, lo:lo + 512], in0=pband[wsl][:, lo:lo + 512],
                                scalar1=-PI, scalar2=TWO_PI, op0=ALU.is_lt, op1=ALU.mult))
                        for lo in (0, 512):
                            P.em("vector", lambda v, lo=lo, wsl=wsl: v.tensor_add(
                                out=ywr[wsl][:, lo:lo + 512], in0=pband[wsl][:, lo:lo + 512],
                                in1=cwr[wsl][:, lo:lo + 512]))
                        for lo in (0, 512):
                            P.em("vector", lambda v, lo=lo, wsl=wsl: v.tensor_scalar(
                                out=cwr[wsl][:, lo:lo + 512], in0=ywr[wsl][:, lo:lo + 512],
                                scalar1=-HALF_PI, scalar2=TWO_PI, op0=ALU.is_lt, op1=ALU.mult))
                        for lo in (0, 512):
                            def waz(v, lo=lo, wsl=wsl, last=(lo == 512)):
                                r3 = v.tensor_add(out=yar[wsl][:, lo:lo + 512], in0=ywr[wsl][:, lo:lo + 512],
                                                  in1=cwr[wsl][:, lo:lo + 512])
                                if last:
                                    r3.then_inc(P.sems["V"], 1)
                            P.em("vector", waz)
                        wrp = P.inc("V")
                        GUARD[f"pband{wsl}"] = {"V": wrp}

                        def flush_sin(gr=gr, ksl=ksl, wsl=wsl, wrp=wrp):
                            P.wait("scalar", "V", wrp)
                            if gr >= 4:
                                P.wait("scalar", "T", kv_round[gr - 4])
                            def snS(s, ksl=ksl, wsl=wsl):
                                s.activation(out=bass.AP(kpA, ksl * 8 * 256 + 128,
                                                         [[H * 512, 128], [256, 8], [1, 128]]),
                                             in_=ywr[wsl][:, 0:1024], func=AF.Sin)
                            def snC(s, ksl=ksl, wsl=wsl):
                                s.activation(out=bass.AP(kpA, ksl * 8 * 256,
                                                         [[H * 512, 128], [256, 8], [1, 128]]),
                                             in_=yar[wsl][:, 0:1024], func=AF.Sin,
                                             bias=nph_c[:, 0:1]).then_inc(P.sems["S"], 1)
                            P.em("scalar", snS)
                            P.em("scalar", snC)
                            sin_done[gr] = P.inc("S")
                            GUARD[f"ywr{wsl}"] = {"S": sin_done[gr]}
                        snq.append(flush_sin)

                        def flush_kv(v_done=v_done, gr=gr, r=r, ti=ti, g=g,
                                     ksl=ksl, vsl=vsl, b=b):
                            P.wait("tensor", "S", sin_done[gr])
                            P.wait("tensor", "V", v_done)
                            if gr == 0:
                                g_wait("tensor", "pKVa")
                                g_wait("tensor", "pKVb")
                            if b == 1 and gr == 0:
                                P.wait("tensor", "S", kv_evac["S"])
                            for hh in range(8):
                                h = r * 8 + hh
                                first = (gr < 2 and (h % 4) == 0)
                                lastt = (ti == GT - 1 and g == NG - 1 and (h % 4) == 3)
                                def mmKV1(t, h=h, hh=hh, ksl=ksl, vsl=vsl, first=first):
                                    t.matmul(out=pKV[:, h * 128: h * 128 + 64],
                                             lhsT=kpA[:, ksl * 2048 + hh * 256: ksl * 2048 + hh * 256 + 128],
                                             rhs=vA[vsl][:, h * 64:(h + 1) * 64],
                                             start=first, stop=False)
                                def mmKV2(t, h=h, hh=hh, ksl=ksl, vsl=vsl, lastt=lastt, last=(hh == 7)):
                                    r2 = t.matmul(out=pKV[:, h * 128 + 64: h * 128 + 128],
                                                  lhsT=kpA[:, ksl * 2048 + hh * 256 + 128: ksl * 2048 + hh * 256 + 256],
                                                  rhs=vA[vsl][:, h * 64:(h + 1) * 64],
                                                  start=False, stop=lastt)
                                    if last:
                                        r2.then_inc(P.sems["T"], 1)
                                P.em("tensor", mmKV1)
                                P.em("tensor", mmKV2)
                            kv_round[gr] = P.inc("T")
                            GUARD[f"vA{vsl}"] = {"T": kv_round[gr]}
                        kvq.append(flush_kv)
                while len(kvq) > 4:
                    kvq.popleft()()
                while snq:
                    snq.popleft()()
                GUARD["kqB"] = {"T": P.cnt["T"]}
                if stat_it is not None:
                    for _ in range(GT):
                        j = next(stat_it, None)
                        if j is not None:
                            emit_stat_tile(j, xres, "R%d")
            while kvq:
                kvq.popleft()()
            P.wait("scalar", "T", P.cnt["T"])
            g_wait("scalar", "kv_s")
            for q in range(4):
                def evKV(s, q=q, last=(q == 3)):
                    r = s.activation(out=kv_s[:, q * 512:(q + 1) * 512],
                                     in_=pKV[:, q * 512:(q + 1) * 512],
                                     func=AF.Identity, scale=c128_c[:, 0:1])
                    if last:
                        r.then_inc(P.sems["S"], 1)
                P.em("scalar", evKV)
            kv_evac["S"] = P.inc("S")

        store_par = {0: 0, 1: 0}

        def emit_D(b, skip_wB=False):
            GUARD["pKVa"] = {"S": kv_evac["S"]}
            GUARD["pKVb"] = {"S": kv_evac["S"]}
            if not skip_wB:
                emit_wload("B", wo_h)
            wlB = wload_cnt["B"]
            import collections
            apq = collections.deque()
            snqD = collections.deque()
            pend = [None]
            last_ffn = [0]

            def emit_subtile(g0_i0, st_i, rnd_c, apev, gp):
                i = g0_i0 + st_i
                par = st_i % 2
                g_wait("sync", f"xres{par}")
                P.em("sync", lambda e, i=i, par=par: e.dma_start(
                    out=xres[par][:, :], in_=x_h[i * 128:(i + 1) * 128, :]).then_inc(P.sems["R%d" % par], 16))
                xrc = P.inc("R%d" % par, 16)
                P.wait("tensor", "S", apev)
                P.wait("tensor", "W", wlB)
                g_wait("tensor", "pT")
                for jc2 in range(2):
                    off = jc2 * 512
                    for cc in range(8):
                        def mmW(t, st_i=st_i, jc2=jc2, cc=cc, off=off, gp=gp, last=(jc2 == 1 and cc == 7)):
                            r2 = t.matmul(out=pT[:, off:off + 512],
                                          lhsT=attnT[gp][:, cc * 512 + st_i * 128: cc * 512 + st_i * 128 + 128],
                                          rhs=wB_s[:, cc * D + jc2 * 512: cc * D + jc2 * 512 + 512],
                                          start=(cc == 0), stop=(cc == 7))
                            if last:
                                r2.then_inc(P.sems["T"], 1)
                        P.em("tensor", mmW)
                tcw = P.inc("T")
                P.wait("vector", "T", tcw)
                g_wait("vector", "woS")
                for jc2 in range(2):
                    def evW(v, lo=jc2 * 512, last=(jc2 == 1)):
                        r2 = v.tensor_add(out=woS[:, lo:lo + 512], in0=pT[:, lo:lo + 512],
                                          in1=cb_b[:, lo:lo + 512])
                        if last:
                            r2.then_inc(P.sems["V"], 1)
                    P.em("vector", evW)
                wv_c = P.inc("V")
                P.wait("tensor", "V", rnd_c)
                P.wait("tensor", "V", wv_c)
                for jc2 in range(2):
                    off = jc2 * 512
                    for dc in range(8):
                        def mmF(t, st_i=st_i, jc2=jc2, dc=dc, off=off, last=(jc2 == 1 and dc == 7)):
                            r2 = t.matmul(out=pT[:, off:off + 512],
                                          lhsT=xi8B[:, dc * 512 + st_i * 128: dc * 512 + st_i * 128 + 128],
                                          rhs=wf_s[:, dc * D + jc2 * 512: dc * D + jc2 * 512 + 512],
                                          start=(dc == 0), stop=(dc == 7))
                            if last:
                                r2.then_inc(P.sems["T"], 1)
                        P.em("tensor", mmF)
                tcf = P.inc("T")
                P.wait("vector", "T", tcf)
                P.wait("vector", "R%d" % par, xrc)
                P.wait("vector", "O%d" % par, store_par[par])
                g_wait("vector", f"xout{par}")
                for jc2 in range(2):
                    P.em("vector", lambda v, par=par, lo=jc2 * 512: v.tensor_mul(
                        out=xout[par][:, lo:lo + 512], in0=pT[:, lo:lo + 512], in1=swsc_b[:, lo:lo + 512]))
                for jc2 in range(2):
                    P.em("vector", lambda v, par=par, lo=jc2 * 512: v.tensor_add(
                        out=xout[par][:, lo:lo + 512], in0=woS[:, lo:lo + 512], in1=xout[par][:, lo:lo + 512]))
                ep_half = {}
                for jc2 in range(2):
                    def ep3(v, par=par, lo=jc2 * 512):
                        v.tensor_add(out=xout[par][:, lo:lo + 512], in0=xres[par][:, lo:lo + 512],
                                     in1=xout[par][:, lo:lo + 512]).then_inc(P.sems["V"], 1)
                    P.em("vector", ep3)
                    ep_half[jc2] = P.inc("V")
                GUARD["pT"] = {"V": ep_half[0]}
                GUARD["woS"] = {"V": ep_half[1]}
                GUARD[f"xres{par}"] = {"V": ep_half[1]}
                P.wait("sync", "V", ep_half[1])
                P.em("sync", lambda e, i=i, par=par: e.dma_start(
                    out=y_h[i * 128:(i + 1) * 128, :], in_=xout[par][:, :]).then_inc(P.sems["O%d" % par], 16))
                store_par[par] = P.inc("O%d" % par, 16)
                return tcf

            for g in range(NG):
                i0 = b * NT + g * GT
                xd = None
                for ti in range(GT):
                    xd = emit_xn_tile(i0 + ti, ti)
                emit_kq_group(wA_s, bq_s, xd, wl=wload_cnt["A"])
                if g == NG - 1 and b == 0:
                    emit_wload("A", wk_h)
                if g + 1 < NG:
                    emit_x_load(i0 + GT)
                    emit_x_load(i0 + GT + 1)
                qpm_ev = {}
                qsin_cnt = {}
                ap_nop = {}
                ap_ev = {}
                for c in range(8):
                    reg = (c % 2) * 1024
                    rname = "pKVa" if c % 2 == 0 else "pKVb"
                    par = c % 4
                    g_wait("tensor", rname)
                    if c >= 2:
                        P.wait("tensor", "S", qpm_ev[c - 2])
                    for hh in range(2):
                        h = 2 * c + hh
                        def mmQ(t, h=h, reg=reg, hh=hh, last=(hh == 1)):
                            r2 = t.matmul(out=pKV[:, reg + hh * 512: reg + hh * 512 + 512],
                                          lhsT=proj_z[h % 2][:, :],
                                          rhs=kqB[:, (h // 2) * 512:(h // 2) * 512 + 512],
                                          start=True, stop=True)
                            if last:
                                r2.then_inc(P.sems["T"], 1)
                        P.em("tensor", mmQ)
                    tc = P.inc("T")
                    if len(apq) >= 3:
                        apq.popleft()()
                    P.wait("scalar", "T", tc)
                    g_wait("scalar", f"pband{par}")
                    P.em("scalar", lambda s, par=par, reg=reg: s.activation(
                        out=pband[par][:, :], in_=pKV[:, reg: reg + 1024],
                        func=AF.Identity, bias=npi_c[:, 0:1]).then_inc(P.sems["S"], 1))
                    qpm_ev[c] = P.inc("S")
                    GUARD[rname] = {"S": qpm_ev[c]}
                    if snqD:
                        snqD.popleft()()
                    P.wait("vector", "S", qpm_ev[c])
                    g_wait("vector", f"ywr{par}")
                    for lo in (0, 512):
                        P.em("vector", lambda v, par=par, lo=lo: v.tensor_scalar(
                            out=cwr[par][:, lo:lo + 512], in0=pband[par][:, lo:lo + 512],
                            scalar1=-PI, scalar2=TWO_PI, op0=ALU.is_lt, op1=ALU.mult))
                    for lo in (0, 512):
                        P.em("vector", lambda v, par=par, lo=lo: v.tensor_add(
                            out=ywr[par][:, lo:lo + 512], in0=pband[par][:, lo:lo + 512],
                            in1=cwr[par][:, lo:lo + 512]))
                    for lo in (0, 512):
                        P.em("vector", lambda v, par=par, lo=lo: v.tensor_scalar(
                            out=cwr[par][:, lo:lo + 512], in0=ywr[par][:, lo:lo + 512],
                            scalar1=-HALF_PI, scalar2=TWO_PI, op0=ALU.is_lt, op1=ALU.mult))
                    for lo in (0, 512):
                        def waz(v, par=par, lo=lo, last=(lo == 512)):
                            r3 = v.tensor_add(out=yar[par][:, lo:lo + 512], in0=ywr[par][:, lo:lo + 512],
                                              in1=cwr[par][:, lo:lo + 512])
                            if last:
                                r3.then_inc(P.sems["V"], 1)
                        P.em("vector", waz)
                    wrp = P.inc("V")
                    GUARD[f"pband{par}"] = {"V": wrp}
                    qsl = c % 3

                    def flush_qsin(c=c, par=par, qsl=qsl, wrp=wrp):
                        P.wait("scalar", "V", wrp)
                        if c >= 3:
                            P.wait("scalar", "T", ap_nop[c - 3])
                        else:
                            g_wait("scalar", f"qp{qsl}")
                        P.em("scalar", lambda s, par=par, qsl=qsl: s.activation(
                            out=qp_sin[qsl][:, :], in_=ywr[par][:, 0:1024], func=AF.Sin))
                        P.em("scalar", lambda s, par=par, qsl=qsl: s.activation(
                            out=qp_cos[qsl][:, :], in_=yar[par][:, 0:1024], func=AF.Sin,
                            bias=nph_c[:, 0:1]).then_inc(P.sems["S"], 1))
                        qsin_cnt[c] = P.inc("S")
                        GUARD[f"ywr{par}"] = {"S": qsin_cnt[c]}
                    snqD.append(flush_qsin)
                    def flush_ap(c=c, qsl=qsl, gp=g % 2):
                        abank = pQK if c % 2 == 0 else pP
                        abname = "pQK" if c % 2 == 0 else "pP"
                        P.wait("tensor", "S", qsin_cnt[c])
                        if c >= 2:
                            P.wait("tensor", "S", ap_ev[c - 2])
                        else:
                            g_wait("tensor", abname)
                        for hh in range(2):
                            h = 2 * c + hh
                            tp = None if hh == 0 else (0, 64)
                            po = 0 if hh == 0 else 64
                            def mmA1(t, h=h, hh=hh, qsl=qsl, abank=abank, tp=tp, po=po):
                                t.matmul(out=abank[po:po + 64, 0:512],
                                         lhsT=kv_s[:, h * 128: h * 128 + 64],
                                         rhs=qp_cos[qsl][:, hh * 512: hh * 512 + 512],
                                         start=True, stop=False, tile_position=tp)
                            def mmA2(t, h=h, hh=hh, qsl=qsl, abank=abank, tp=tp, po=po):
                                r2 = t.matmul(out=abank[po:po + 64, 0:512],
                                              lhsT=kv_s[:, h * 128 + 64: h * 128 + 128],
                                              rhs=qp_sin[qsl][:, hh * 512: hh * 512 + 512],
                                              start=False, stop=True, tile_position=tp)
                                if hh == 1:
                                    r2.then_inc(P.sems["T"], 1)
                            P.em("tensor", mmA1)
                            P.em("tensor", mmA2)
                        ap_nop[c] = P.inc("T")
                        P.wait("scalar", "T", ap_nop[c])
                        g_wait("scalar", f"attnT{gp}")
                        def evA(s, c=c, abank=abank, gp=gp):
                            s.activation(out=attnT[gp][:, c * 512:(c + 1) * 512],
                                         in_=abank[:, 0:512], func=AF.Copy).then_inc(P.sems["S"], 1)
                        P.em("scalar", evA)
                        ap_ev[c] = P.inc("S")
                        GUARD[abname] = {"S": ap_ev[c]}
                    apq.append(flush_ap)
                    if pend[0] is not None and c % 2 == 1:
                        pg_i0, pg_rnd, pg_apev, pg_gp = pend[0]
                        last_ffn[0] = emit_subtile(pg_i0, c // 2, pg_rnd, pg_apev, pg_gp)
                        if c == 7:
                            GUARD[f"attnT{pg_gp}"] = {"T": last_ffn[0]}
                            pend[0] = None
                while snqD:
                    snqD.popleft()()
                while apq:
                    apq.popleft()()
                for _j in range(3):
                    GUARD[f"qp{_j}"] = {"T": P.cnt["T"]}
                g_wait("vector", "xi8B")
                P.wait("vector", "T", last_ffn[0])
                for dc in range(8):
                    P.em("vector", lambda v, dc=dc: v.tensor_scalar(
                        out=rtmp[:, 0:512],
                        in0=xnB[:, dc * 512:(dc + 1) * 512],
                        scalar1=invs_c[:, 0:1], scalar2=MAGIC, op0=ALU.mult, op1=ALU.add))
                    def rnd2(v, dc=dc, last=(dc == 7)):
                        r = v.tensor_scalar(out=xi8B[:, dc * 512:(dc + 1) * 512],
                                            in0=rtmp[:, 0:512],
                                            scalar1=MAGIC, scalar2=None, op0=ALU.subtract)
                        if last:
                            r.then_inc(P.sems["V"], 1)
                    P.em("vector", rnd2)
                rnd_cnt = P.inc("V")
                GUARD["xnB_sc"] = {"V": rnd_cnt}
                GUARD["xnB_v"] = {"V": rnd_cnt}
                pend[0] = (i0, rnd_cnt, ap_ev[7], g % 2)
            pg_i0, pg_rnd, pg_apev, pg_gp = pend[0]
            for st_i in range(GT):
                last_ffn[0] = emit_subtile(pg_i0, st_i, pg_rnd, pg_apev, pg_gp)
            pend[0] = None
            GUARD[f"attnT{pg_gp}"] = {"T": last_ffn[0]}
            GUARD["xi8B"] = {"T": last_ffn[0]}

        emit_C(0, stat_it=iter(range(NT, NTILES)))
        GUARD["xout0"] = {"S": a_sc[NTILES - 2]}
        GUARD["xout1"] = {"S": a_sc[NTILES - 1]}
        emit_wload("B", wo_h)
        emit_B1(NT, NTILES)
        emit_B2()
        emit_D(0, skip_wB=True)
        emit_C(1)
        emit_D(1)

        P.wait("sync", "O0", P.cnt["O0"])
        P.wait("sync", "O1", P.cnt["O1"])

        block = S.enter_context(nc.Block())

        @block.sync
        def _(sync):
            for f in P.q["sync"]:
                f(sync)

        @block.vector
        def _(vector):
            for f in P.q["vector"]:
                f(vector)

        @block.scalar
        def _(scalar):
            for f in P.q["scalar"]:
                f(scalar)

        @block.tensor
        def _(tensor):
            for f in P.q["tensor"]:
                f(tensor)

        @block.gpsimd
        def _(gpsimd):
            for f in P.q["gpsimd"]:
                f(gpsimd)

    return nc


def _prep_w(w):
    return np.ascontiguousarray(w.reshape(8, 128, 1024).transpose(1, 0, 2).reshape(128, 8192))


_NC_CACHE = {}


def kernel(L0_attn_x, L0_attn_proj_mat, L0_attn_Wq, L0_attn_bq, L0_attn_Wk, L0_attn_bk,
           L0_attn_Wv, L0_attn_bv, L0_attn_Wo, L0_attn_bo,
           L0_ffn_weight_int8, L0_ffn_weight_scale, L0_ffn_bias):
    x = np.ascontiguousarray(np.asarray(L0_attn_x, dtype=np.float32))
    wq = _prep_w(np.ascontiguousarray(np.asarray(L0_attn_Wq, dtype=np.float32).T))
    wk = _prep_w(np.ascontiguousarray(np.asarray(L0_attn_Wk, dtype=np.float32).T))
    wv = _prep_w(np.ascontiguousarray(np.asarray(L0_attn_Wv, dtype=np.float32).T))
    wo = _prep_w(np.ascontiguousarray(np.asarray(L0_attn_Wo, dtype=np.float32).T))
    wf = _prep_w(np.ascontiguousarray(np.asarray(L0_ffn_weight_int8).T))
    proj = np.ascontiguousarray(np.vstack([np.asarray(L0_attn_proj_mat, dtype=np.float32)] * 2))
    bq = np.ascontiguousarray(np.asarray(L0_attn_bq, dtype=np.float32).reshape(8, 128).T)
    bk = np.ascontiguousarray(np.asarray(L0_attn_bk, dtype=np.float32).reshape(8, 128).T)
    bv = np.asarray(L0_attn_bv, dtype=np.float32).reshape(1, D)
    bo = np.asarray(L0_attn_bo, dtype=np.float32).reshape(1, D)
    fb = np.asarray(L0_ffn_bias, dtype=np.float32).reshape(1, D)
    wsc = np.ascontiguousarray(np.asarray(L0_ffn_weight_scale, dtype=np.float32).reshape(1, D))
    ident = np.eye(128, dtype=np.float32)

    if "nc" not in _NC_CACHE:
        _NC_CACHE["nc"] = build_nc()
    nc = _NC_CACHE["nc"]

    in_maps = []
    for c in range(8):
        xs = np.ascontiguousarray(x[2 * c:2 * c + 2].reshape(NB * TOK, D))
        in_maps.append({
            "x": xs, "wq": wq, "wk": wk, "wv": wv, "wo": wo, "wf": wf,
            "proj": proj, "bq": bq, "bk": bk, "bv": bv, "bo": bo,
            "fb": fb, "wsc": wsc, "id_h": ident,
        })
    res = run_bass_kernel_spmd(nc, in_maps, core_ids=list(range(8)))
    out = np.empty((16, TOK, D), dtype=np.float32)
    for c in range(8):
        out[2 * c:2 * c + 2] = res.results[c]["y"].reshape(2, TOK, D)
    return out
